# version 06
# baseline (speedup 1.0000x reference)
import numpy as np
import ml_dtypes
from contextlib import ExitStack

import concourse.bass as bass
import concourse.mybir as mybir
from concourse.bass_utils import run_bass_kernel_spmd

F32 = mybir.dt.float32
BF16 = mybir.dt.bfloat16
AF = mybir.ActivationFunctionType
ALU = mybir.AluOpType
AX = mybir.AxisListType

MAGIC = 12582912.0

L = 4096
D = 1024
H = 2048
E = 2 * H
NS = 16
DR = 64
NI = 2816
NB = 8

T = 512
NCH = L // T
DC = D // 128
HC = H // 128
ECN = E // 128
ICN = NI // 128
UE = T + 2
NG = 8
GH = HC // NG

ENGS = ("PE", "ACT", "DVE", "POOL", "SP")


class Builder:
    def __init__(self):
        self.items = []
        self.dma_counts = {"SP": 0, "ACT": 0, "POOL": 0}
        self.pending_barrier = {e: None for e in ENGS}
        self.last_item = {e: None for e in ENGS}

    @staticmethod
    def _flat(deps):
        out = []
        for d in deps:
            if d is None:
                continue
            if isinstance(d, list):
                out.extend(x for x in d if x is not None)
            else:
                out.append(d)
        return out

    def ring_token(self, ring):
        return ("D", ring, self.dma_counts[ring])

    def add(self, eng, fn, deps=()):
        deps = self._flat(deps)
        if self.pending_barrier[eng]:
            deps.extend(self.pending_barrier[eng])
            self.pending_barrier[eng] = None
        idx = len(self.items)
        for d in deps:
            assert d[0] in ("i", "d", "D"), d
            assert d[0] != "i" or d[1] < idx
        self.items.append([eng, deps, ("x", fn), False])
        tok = ("i", idx, eng)
        self.last_item[eng] = tok
        return tok

    def dma(self, eng, out, in_, deps=(), **kw):
        deps = self._flat(deps)
        if self.pending_barrier[eng]:
            deps.extend(self.pending_barrier[eng])
            self.pending_barrier[eng] = None
        ring = eng
        self.dma_counts[ring] += 1
        cnt = self.dma_counts[ring]
        idx = len(self.items)

        def fn(e, _o=out, _i=in_, _kw=kw):
            return e.dma_start(out=_o, in_=_i, **_kw)
        self.items.append([eng, deps, ("dma", fn, ring), False])
        tok = ("i", idx, eng)
        self.last_item[eng] = tok
        return tok, ("d", ring, cnt)

    def barrier(self):
        rings = [self.ring_token(r) for r in ("SP", "ACT", "POOL")]
        toks = [self.add(e, None, [self.last_item[e]] + rings) for e in ENGS]
        for e in ENGS:
            self.pending_barrier[e] = [t for t in toks if t[2] != e]

    def emit(self, nc, ctx):
        n = len(self.items)
        for eng, deps, fn, _ in self.items:
            for d in deps:
                if d[0] == "i":
                    self.items[d[1]][3] = True
        inc_after = [0] * n
        counters = {e: 0 for e in ENGS}
        for i, (eng, deps, fn, needs) in enumerate(self.items):
            if needs:
                counters[eng] += 1
            inc_after[i] = counters[eng]

        NSEM = 16
        prog = {e: ctx.enter_context(nc.semaphore(f"prog{e}")) for e in ENGS}
        dsem = {r: [ctx.enter_context(nc.semaphore(f"dma{r}{k}")) for k in range(NSEM)]
                for r in ("SP", "ACT", "POOL")}
        items = self.items
        dma_index = {}
        nxt = {r: 0 for r in ("SP", "ACT", "POOL")}
        for i, (eng, deps, fn, needs) in enumerate(items):
            if fn[0] == "dma":
                dma_index[i] = nxt[fn[2]]
                nxt[fn[2]] += 1

        def dtok_semval(ring, cnt):
            j = cnt - 1
            return dsem[ring][j % NSEM], 16 * (j // NSEM + 1)

        def run_engine(eobj, eng):
            last_wait = {}
            for i, (e2, deps, fn, needs) in enumerate(items):
                if e2 != eng:
                    continue
                for d in deps:
                    if d[0] == "i":
                        j = d[1]
                        pe = items[j][0]
                        sw = [(prog[pe], inc_after[j])]
                    elif d[0] == "d":
                        sw = [dtok_semval(d[1], d[2])]
                    else:
                        cnt = d[2]
                        sw = []
                        for k in range(min(cnt, NSEM)):
                            nk = (cnt - 1 - k) // NSEM + 1
                            sw.append((dsem[d[1]][k], 16 * nk))
                    for sem, val in sw:
                        if last_wait.get(sem.name, -1) >= val:
                            continue
                        eobj.wait_ge(sem, val)
                        last_wait[sem.name] = val
                if fn[0] == "dma":
                    j = dma_index[i]
                    sem = dsem[fn[2]][j % NSEM]
                    prev_use = j // NSEM
                    if prev_use > 0 and last_wait.get(sem.name, -1) < 16 * prev_use:
                        eobj.wait_ge(sem, 16 * prev_use)
                        last_wait[sem.name] = 16 * prev_use
                    ins = fn[1](eobj)
                    ins.then_inc(sem, 16)
                    if needs:
                        eobj.nop(nofuse=True).then_inc(prog[eng], 1)
                else:
                    f = fn[1]
                    ins = eobj.nop(nofuse=True) if f is None else f(eobj)
                    if needs:
                        ins.then_inc(prog[eng], 1)

        block = ctx.enter_context(nc.Block())
        block.tensor(lambda e: run_engine(e, "PE"))
        block.scalar(lambda e: run_engine(e, "ACT"))
        block.vector(lambda e: run_engine(e, "DVE"))
        block.gpsimd(lambda e: run_engine(e, "POOL"))
        block.sync(lambda e: run_engine(e, "SP"))


def rep128(src_ap, count=128):
    return bass.AP(src_ap.tensor, src_ap.offset,
                   [[0, count]] + [list(p) for p in src_ap.ap])


def ap3(full_ap, off, dims):
    return bass.AP(full_ap.tensor, full_ap.offset + off,
                   [list(full_ap.ap[0])] + [list(d) for d in dims])


def hap(full_ap, off, dims):
    return bass.AP(full_ap.tensor, full_ap.offset + off, [list(d) for d in dims])


def build(stop_after=None):
    nc = bass.Bass("TRN2", target_bir_lowering=False, debug=False)
    S = Builder()

    def hin(name, shape, dtype=F32):
        return nc.dram_tensor(name, shape, dtype, kind="ExternalInput").ap()

    x_h = hin("x", [D, L])
    w_inT_h = hin("w_inT", [D, E])
    convw_h = hin("convw", [128, HC * 3], BF16)
    convb_h = hin("convb", [128, HC])
    wxpT_h = hin("wxpT", [H, 96])
    wdtT_h = hin("wdtT", [DR, H])
    bdt_h = hin("bdt", [128, HC])
    alog_h = hin("alog", [1, NS])
    dvec_h = hin("dvec", [128, HC])
    woutT_h = hin("woutT", [H, D])
    clipx_h = hin("clipx", [1, 1])
    clipd_h = hin("clipd", [1, 1])
    wg_h = hin("wg", [NI, D])
    wu_h = hin("wu", [NI, D])
    wd_h = hin("wd", [D, NI])
    wgT_h = hin("wgT", [D, NI])
    wuT_h = hin("wuT", [D, NI])
    wdT_h = hin("wdT", [NI, D])
    cvec_h = hin("cvec", [128, DC])
    wmod_h = hin("wmod", [D, 6 * D])
    onesb_h = hin("onesb", [128, 1], BF16)

    out_h = nc.dram_tensor("out", [D, L], F32, kind="ExternalOutput").ap()

    bc_h = nc.dram_tensor("bc_s", [2 * NS, L], BF16, kind="Internal").ap()
    z_h = nc.dram_tensor("z_s", [H, L], BF16, kind="Internal").ap()
    x2_h = nc.dram_tensor("x2_s", [D, L], F32, kind="Internal").ap()
    h2_h = nc.dram_tensor("h2_s", [D, L], F32, kind="Internal").ap()
    p_h = nc.dram_tensor("p_s", [NI, L], F32, kind="Internal").ap()
    st1_h = nc.dram_tensor("st1_s", [1, 2 * L], BF16, kind="Internal").ap()
    st2_h = nc.dram_tensor("st2_s", [1, 2 * L], F32, kind="Internal").ap()
    mod_h = nc.dram_tensor("mod_s", [1, 6 * D], F32, kind="Internal").ap()
    sc_h = nc.dram_tensor("sc_s", [1, 256], F32, kind="Internal").ap()
    rsg_h = nc.dram_tensor("rsg_s", [1, NI], F32, kind="Internal").ap()
    rsu_h = nc.dram_tensor("rsu_s", [1, NI], F32, kind="Internal").ap()
    rsd_h = nc.dram_tensor("rsd_s", [1, D], F32, kind="Internal").ap()
    cc_in1 = nc.dram_tensor("cc_in1", [1, 1], F32, kind="Internal").ap()
    cc_out1 = nc.dram_tensor("cc_out1", [1, 8], F32, kind="Internal").ap()
    cc_in2 = nc.dram_tensor("cc_in2", [1, 1], F32, kind="Internal").ap()
    cc_out2 = nc.dram_tensor("cc_out2", [1, 8], F32, kind="Internal").ap()

    with ExitStack() as ctx:
        def SB(scope, name, shape, dt=F32):
            return scope.enter_context(nc.sbuf_tensor(name, shape, dt))

        pb = [ctx.enter_context(nc.psum_tensor(f"pb{i}", [128, 512], F32)) for i in range(8)]

        onesb = SB(ctx, "onesb_s", [128, 1], BF16)
        onesf = SB(ctx, "onesf_s", [128, 1])
        convw = SB(ctx, "convw_s", [128, HC * 3], BF16)
        convb = SB(ctx, "convb_s", [128, HC])
        bdt = SB(ctx, "bdt_s", [128, HC])
        dvec = SB(ctx, "dvec_s", [128, HC])
        a_bc = SB(ctx, "a_bc", [128, NS])
        cvec = SB(ctx, "cvec_s", [128, DC])
        siluc = SB(ctx, "siluc", [128, DC])
        modc = SB(ctx, "modc", [128, 6 * DC])
        sxr = SB(ctx, "sxr", [128, 8])
        small = SB(ctx, "small", [1, 64])
        cst1 = SB(ctx, "cst1", [1, 4])
        state = SB(ctx, "state", [128, NS * HC])
        rmaxb = SB(ctx, "rmaxb", [128, 1])
        rpx = SB(ctx, "rpx", [128, 1])
        csc = SB(ctx, "csc", [128, DC])
        wxpT = SB(ctx, "wxpT_s", [128, HC * 96], BF16)
        wdtT = SB(ctx, "wdtT_s", [64, H], BF16)
        wmaxg = SB(ctx, "wmaxg", [128, ICN])
        wmaxu = SB(ctx, "wmaxu", [128, ICN])
        wmaxd = SB(ctx, "wmaxd", [128, DC])
        wsceg = SB(ctx, "wsceg", [128, ICN])
        wsceu = SB(ctx, "wsceu", [128, ICN])

        SL_ALOG, SL_A, SL_CX, SL_CD, SL_GM, SL_SX, SL_RX, SL_T0 = 0, 16, 33, 34, 35, 36, 37, 38
        SL_GM2, SL_SD, SL_RD, SL_T1, SL_SX127, SL_SD127 = 40, 41, 42, 43, 44, 45
        SL_CC = 48

        p0 = ctx.enter_context(ExitStack())
        wmod_buf = SB(p0, "wmod_buf", [128, 1024])
        modsb = SB(p0, "modsb", [1, 6 * D])
        wrow = [SB(p0, f"wrow{i}", [128, NI]) for i in range(2)]

        for _i, _v in ((0, 1e-5), (1, -0.5), (2, -1.0)):
            t_cst = S.add("DVE", lambda e, _a=_i, _b=_v: e.memset(cst1[0:1, _a:_a + 1], _b))
        t_onesf = S.add("DVE", lambda e: e.memset(onesf[:, :], 1.0))

        tload = {}
        for nm, dst, src in (("onesb", onesb, onesb_h), ("convw", convw, convw_h),
                             ("convb", convb, convb_h), ("bdt", bdt, bdt_h),
                             ("dvec", dvec, dvec_h), ("cvec", cvec, cvec_h)):
            _, tload[nm] = S.dma("SP", dst[:, :], src)
        _, tload["wxpT"] = S.dma("POOL", wxpT[:, :],
                                 hap(wxpT_h, 0, [[96, 128], [128 * 96, HC], [1, 96]]))
        _, tload["wdtT"] = S.dma("POOL", wdtT[0:64, :], wdtT_h)
        _, t_alog = S.dma("SP", small[0:1, SL_ALOG:SL_ALOG + NS], alog_h)
        _, t_cx = S.dma("SP", small[0:1, SL_CX:SL_CX + 1], clipx_h)
        _, t_cd = S.dma("SP", small[0:1, SL_CD:SL_CD + 1], clipd_h)

        tA = S.add("ACT", lambda e: e.activation(small[0:1, SL_A:SL_A + NS],
                                                 small[0:1, SL_ALOG:SL_ALOG + NS], AF.Exp), [t_alog])
        tA = S.add("DVE", lambda e: e.tensor_scalar(small[0:1, SL_A:SL_A + NS],
                                                    small[0:1, SL_A:SL_A + NS], -1.0, None, ALU.mult), [tA])
        _, dA1 = S.dma("ACT", sc_h[0:1, 0:NS], small[0:1, SL_A:SL_A + NS], [tA])
        _, t_abc = S.dma("SP", a_bc[:, :], rep128(sc_h[0:1, 0:NS]), [dA1])

        t_sc = S.add("ACT", lambda e: e.activation(siluc[:, :], cvec[:, :], AF.Silu), [tload["cvec"]])
        prev_es_ev = None
        for es in range(12):
            mmt = None
            for kc in range(DC):
                b = (kc % 2) * 512
                _, dtok = S.dma("SP", wmod_buf[:, b:b + 512],
                                wmod_h[kc * 128:(kc + 1) * 128, es * 512:(es + 1) * 512],
                                [mmt] if kc >= 2 else [prev_es_ev])
                mmt = S.add("PE", lambda e, _kc=kc, _b=b: e.matmul(
                    pb[6][0:1, :], siluc[:, _kc:_kc + 1], wmod_buf[:, _b:_b + 512],
                    start=(_kc == 0), stop=(_kc == DC - 1)),
                    [dtok, t_sc, prev_es_ev])
            prev_es_ev = S.add("ACT", lambda e, _es=es: e.copy(
                modsb[0:1, _es * 512:(_es + 1) * 512], pb[6][0:1, :]), [mmt])
        _, t_modw = S.dma("ACT", mod_h[0:1, :], modsb[0:1, :], [prev_es_ev])
        for k in range(6):
            for dc in range(DC):
                S.dma("SP", modc[:, k * DC + dc:k * DC + dc + 1],
                      hap(mod_h, k * 1024 + dc * 128, [[1, 128], [1, 1]]),
                      [t_modw])
        t_modr = S.ring_token("SP")
        t_ca = S.add("DVE", lambda e: e.tensor_scalar(modc[:, DC:2 * DC], modc[:, DC:2 * DC], 1.0, None, ALU.add), [t_modr])
        t_cb = S.add("DVE", lambda e: e.tensor_scalar(modc[:, 4 * DC:5 * DC], modc[:, 4 * DC:5 * DC], 1.0, None, ALU.add), [t_modr])
        gfix_toks = []
        for col in (2 * DC, 5 * DC):
            tg1 = S.add("ACT", lambda e, _c=col: e.activation(sxr[:, 0:DC], modc[:, _c:_c + DC], AF.Abs),
                        [t_modr, gfix_toks[-1] if gfix_toks else None])
            tg2 = S.add("DVE", lambda e: e.tensor_scalar(sxr[:, 0:DC], sxr[:, 0:DC], 0.25, None, ALU.max), [tg1])
            tg3 = S.add("DVE", lambda e, _c=col: e.tensor_scalar(csc[:, 0:DC], modc[:, _c:_c + DC], 0.0, 2.0, ALU.is_ge, ALU.mult), [t_modr])
            tg4 = S.add("DVE", lambda e: e.tensor_scalar(csc[:, 0:DC], csc[:, 0:DC], 1.0, None, ALU.subtract), [tg3])
            tg5 = S.add("DVE", lambda e, _c=col: e.tensor_tensor(modc[:, _c:_c + DC], sxr[:, 0:DC], csc[:, 0:DC], ALU.mult), [tg2, tg4])
            gfix_toks.append(tg5)

        rm_last = {0: None, 1: None}
        rmax_done = []
        for wsb, wh in ((wmaxg, wg_h), (wmaxu, wu_h)):
            for ic in range(ICN):
                par = ic % 2
                _, dw = S.dma("POOL", wrow[par][:, 0:D], wh[ic * 128:(ic + 1) * 128, :],
                              [rm_last[par]])
                rm_last[par] = S.add("DVE", lambda e, _w=wsb, _i=ic, _p=par: e.reduce_max(
                    _w[:, _i:_i + 1], wrow[_p][:, 0:D], axis=AX.X, apply_absolute_value=True),
                    [dw])
            rmax_done.append(rm_last[0])
            rmax_done.append(rm_last[1])
        for dc in range(DC):
            par = dc % 2
            _, dw = S.dma("POOL", wrow[par][:, 0:NI], wd_h[dc * 128:(dc + 1) * 128, :],
                          [rm_last[par]])
            rm_last[par] = S.add("DVE", lambda e, _d=dc, _p=par: e.reduce_max(
                wmaxd[:, _d:_d + 1], wrow[_p][:, 0:NI], axis=AX.X, apply_absolute_value=True),
                [dw])
        rmax_done = [rm_last[0], rm_last[1]]
        trg = S.add("DVE", lambda e: e.reciprocal(wsceg[:, :], wmaxg[:, :]), [rmax_done])
        trg = S.add("DVE", lambda e: e.tensor_scalar(
            wsceg[:, :], wsceg[:, :], 127.0, None, ALU.mult), [trg])
        tru = S.add("DVE", lambda e: e.reciprocal(wsceu[:, :], wmaxu[:, :]), [rmax_done])
        tru = S.add("DVE", lambda e: e.tensor_scalar(
            wsceu[:, :], wsceu[:, :], 127.0, None, ALU.mult), [tru])
        trd = S.add("DVE", lambda e: e.reciprocal(csc[:, 0:DC], wmaxd[:, :]), [rmax_done, gfix_toks[-1]])
        trd = S.add("DVE", lambda e: e.tensor_scalar(
            csc[:, 0:DC], csc[:, 0:DC], 127.0, None, ALU.mult), [trd])
        drg = dru = drd = None
        for ic in range(ICN):
            _, drg = S.dma("ACT", hap(rsg_h, ic * 128, [[1, 128]]), wsceg[:, ic:ic + 1], [trg])
            _, dru = S.dma("ACT", hap(rsu_h, ic * 128, [[1, 128]]), wsceu[:, ic:ic + 1], [tru])
        for dc in range(DC):
            _, drd = S.dma("ACT", hap(rsd_h, dc * 128, [[1, 128]]), csc[:, dc:dc + 1], [trd])
        rs_bounced = [drg, dru, drd]

        if stop_after == "P0":
            S.add("SP", lambda e: e.nop(nofuse=True), [S.ring_token("ACT"), S.ring_token("SP"), S.ring_token("POOL")])
            S.emit(nc, ctx)
            return nc
        S.barrier()
        p0.close()

        pm = ctx.enter_context(ExitStack())
        xbf = SB(pm, "xbf", [128, DC * T], BF16)
        xsq = SB(pm, "xsq", [128, T], BF16)
        xsqf = SB(pm, "xsqf", [128, T])
        h1c = [SB(pm, f"h1c{i}", [128, DC * T], BF16) for i in range(2)]
        mrbc = SB(pm, "mrbc", [128, 2 * T], BF16)
        stats = SB(pm, "stats", [1, 3 * T])
        stb16 = SB(pm, "stb16", [1, 2 * T], BF16)
        u_ext = [SB(pm, f"u_ext{i}", [128, HC * UE], BF16) for i in range(2)]
        zsp = [SB(pm, f"zsp{i}", [128, T], BF16) for i in range(2)]
        zld = [SB(pm, f"zld{i}", [128, GH * T], BF16) for i in range(1)]
        ucb = SB(pm, "ucb", [128, HC * T], BF16)
        dtb = SB(pm, "dtb", [128, HC * T], BF16)
        winb = [SB(pm, f"winb{i}", [128, DC * 128], BF16) for i in range(2)]
        wob = [SB(pm, f"wob{i}", [128, HC * 128], BF16) for i in range(2)]
        dtpb = [SB(pm, f"dtpb{i}", [64, T], BF16) for i in range(2)]
        bcb = [SB(pm, f"bcb{i}", [32, T], BF16) for i in range(2)]
        dAb = [SB(pm, f"dAb{i}", [128, GH * T]) for i in range(2)]
        bub = [SB(pm, f"bub{i}", [128, GH * T], BF16) for i in range(2)]
        hsb = [SB(pm, f"hsb{i}", [128, GH * T], BF16) for i in range(2)]
        tmpb = SB(pm, "tmpb", [128, GH * T], BF16)
        ebuf = [SB(pm, f"ebuf{i}", [128, T], BF16) for i in range(2)]
        bbc = [SB(pm, f"bbc{i}", [128, GH * T], BF16) for i in range(2)]
        cbc = [SB(pm, f"cbc{i}", [128, GH * T], BF16) for i in range(2)]
        y_acc = SB(pm, "y_acc", [128, HC * T], BF16)
        x2b = SB(pm, "x2b", [128, DC * T])
        xres = [SB(pm, f"xres{i}", [128, T]) for i in range(2)]
        e1f = [SB(pm, f"e1f{i}", [128, T]) for i in range(2)]
        h2sl = [SB(pm, f"h2sl{i}", [128, T]) for i in range(2)]
        m2bc = SB(pm, "m2bc", [128, 2 * T])

        tzs = S.add("DVE", lambda e: e.memset(state[:, :], 0.0))
        tzr = S.add("DVE", lambda e: e.memset(rmaxb[:, :], 0.0))

        rot = {"i": 0}
        bank_guard = {}

        def next_bank():
            b = rot["i"] % 4
            rot["i"] += 1
            return b, bank_guard.get(b)

        cross = {}
        conv_deps_of = {}

        def stage_A(c):
            t0 = c * T
            cur = c % 2
            _, dx = S.dma("POOL", xbf[:, :],
                          hap(x_h, t0, [[L, 128], [128 * L, DC], [1, T]]),
                          [cross.get("xbf_free")])
            msum = None
            msq = None
            for dc in range(DC):
                tq = S.add("ACT", lambda e, _dc=dc: e.activation(
                    xsq[:, :], xbf[:, _dc * T:(_dc + 1) * T], AF.Square),
                    [dx, cross.get("xsq_free"), msq])
                msum = S.add("PE", lambda e, _dc=dc: e.matmul(
                    pb[4][0:1, :], onesb[:, :], xbf[:, _dc * T:(_dc + 1) * T],
                    start=(_dc == 0), stop=(_dc == DC - 1)),
                    [dx, tload["onesb"], cross.get("b4_free") if dc == 0 else None, msum])
                msq = S.add("PE", lambda e, _dc=dc: e.matmul(
                    pb[5][0:1, :], onesb[:, :], xsq[:, :],
                    start=(_dc == 0), stop=(_dc == DC - 1)),
                    [tq, cross.get("b5_free") if dc == 0 else None, msq])
            cross["xsq_free"] = msq
            tm = S.add("DVE", lambda e: e.tensor_scalar(
                stats[0:1, 0:T], pb[4][0:1, :], 1.0 / D, None, ALU.mult),
                [msum, cross.get("stats_free")])
            cross["b4_free"] = tm
            tmsq = S.add("ACT", lambda e: e.activation(
                stats[0:1, 2 * T:3 * T], stats[0:1, 0:T], AF.Square), [tm])
            tv = S.add("DVE", lambda e: e.scalar_tensor_tensor(
                stats[0:1, T:2 * T], pb[5][0:1, :], 1.0 / D,
                stats[0:1, 2 * T:3 * T], ALU.mult, ALU.subtract), [msq, tmsq])
            cross["b5_free"] = tv
            tl = S.add("ACT", lambda e: e.activation(
                stats[0:1, 2 * T:3 * T], stats[0:1, T:2 * T], AF.Ln,
                bias=cst1[0:1, 0:1]), [tv, t_cst])
            tr = S.add("ACT", lambda e: e.activation(
                stats[0:1, 2 * T:3 * T], stats[0:1, 2 * T:3 * T], AF.Exp,
                scale=cst1[0:1, 1:2]), [tl, t_cst])
            tb1 = S.add("DVE", lambda e: e.tensor_scalar(
                stb16[0:1, 0:T], stats[0:1, 0:T], 1.0, None, ALU.mult), [tm, cross.get("stb_free")])
            tb2 = S.add("DVE", lambda e: e.tensor_scalar(
                stb16[0:1, T:2 * T], stats[0:1, 2 * T:3 * T], 1.0, None, ALU.mult), [tr, cross.get("stb_free")])
            cross["stats_free"] = [tb1, tb2]
            _, dm1 = S.dma("ACT", st1_h[0:1, t0:t0 + T], stb16[0:1, 0:T], [tb1])
            _, dm2 = S.dma("ACT", st1_h[0:1, L + t0:L + t0 + T], stb16[0:1, T:2 * T], [tb2])
            cross["stb_free"] = [dm1, dm2]
            _, dmb1 = S.dma("SP", mrbc[:, 0:T], rep128(hap(st1_h, t0, [[1, T]])),
                            [dm1, cross.get("mrbc_free")])
            _, dmb2 = S.dma("SP", mrbc[:, T:2 * T], rep128(hap(st1_h, L + t0, [[1, T]])),
                            [dm2, cross.get("mrbc_free")])
            tln1 = S.add("DVE", lambda e, _cu=cur: e.tensor_tensor(
                ap3(h1c[_cu][:, :], 0, [[T, DC], [1, T]]),
                ap3(xbf[:, :], 0, [[T, DC], [1, T]]),
                ap3(mrbc[:, :], 0, [[0, DC], [1, T]]), ALU.subtract),
                [dx, dmb1, cross.get(f"h1_free{cur}")])
            cross["xbf_free"] = tln1
            tln2 = S.add("DVE", lambda e, _cu=cur: e.tensor_tensor(
                ap3(h1c[_cu][:, :], 0, [[T, DC], [1, T]]),
                ap3(h1c[_cu][:, :], 0, [[T, DC], [1, T]]),
                ap3(mrbc[:, :], T, [[0, DC], [1, T]]), ALU.mult), [tln1, dmb2])
            cross["mrbc_free"] = tln2
            th1 = []
            for dc in range(DC):
                th1.append(S.add("DVE", lambda e, _dc=dc, _cu=cur: e.tensor_scalar(
                    h1c[_cu][:, _dc * T:(_dc + 1) * T],
                    h1c[_cu][:, _dc * T:(_dc + 1) * T],
                    modc[:, DC + _dc:DC + _dc + 1], modc[:, _dc:_dc + 1],
                    ALU.mult, ALU.add),
                    [tln2, t_ca, t_modr]))
            chunk_state[c] = {"th1": th1, "cur": cur}

        def stage_xz(c, ec0, ec1):
            cur = c % 2
            st = chunk_state[c]
            th1 = st["th1"]
            ev_u = st.setdefault("ev_u", [])
            ev_z = st.setdefault("ev_z", [])
            zdmas = st.setdefault("zdmas", [])
            for ec in range(ec0, ec1):
                wpar = ec % 2
                _, dwt = S.dma("POOL", winb[wpar][:, :],
                               hap(w_inT_h, ec * 128, [[E, 128], [128 * E, DC], [1, 128]]),
                               [cross.get(f"winb_free{wpar}")])
                bk, bg = next_bank()
                mmt = None
                for kc in range(DC):
                    mmt = S.add("PE", lambda e, _ec=ec, _kc=kc, _w=wpar, _b=bk, _cu=cur: e.matmul(
                        pb[_b][:, :],
                        winb[_w][:, _kc * 128:(_kc + 1) * 128],
                        h1c[_cu][:, _kc * T:(_kc + 1) * T],
                        start=(_kc == 0), stop=(_kc == DC - 1)),
                        [th1[kc], dwt, bg if kc == 0 else None, mmt])
                if ec < HC:
                    tev = S.add("ACT", lambda e, _ec=ec, _b=bk, _cu=cur: e.copy(
                        u_ext[_cu][:, _ec * UE + 1:_ec * UE + 1 + T], pb[_b][:, :]),
                        [mmt, cross.get(f"uext_free{cur}") if ec == 0 else None])
                    ev_u.append(tev)
                else:
                    zpar = ec % 2
                    tev = S.add("ACT", lambda e, _b=bk, _z=zpar: e.copy(
                        zsp[_z][:, :], pb[_b][:, :]),
                        [mmt, cross.get(f"zsp_free{zpar}")])
                    _, dz = S.dma("ACT", z_h[(ec - HC) * 128:(ec - HC + 1) * 128,
                                             c * T:c * T + T],
                                  zsp[zpar][:, :], [tev])
                    cross[f"zsp_free{zpar}"] = dz
                    zdmas.append(dz)
                bank_guard[bk] = tev
                cross[f"winb_free{wpar}"] = mmt
            if ec1 == ECN:
                cross[f"z_spilled{cur}"] = zdmas
                cross["uext_prev_done_full"] = ev_u[-1]

        def stage_edges(c):
            cur = c % 2
            ev_u = chunk_state[c]["ev_u"]
            if c == 0:
                tle = S.add("DVE", lambda e, _cu=cur: e.memset(
                    ap3(u_ext[_cu][:, :], 0, [[UE, HC], [1, 1]]), 0.0),
                    [cross.get(f"uext_free{cur}")])
            else:
                tle = S.add("DVE", lambda e, _cu=cur: e.tensor_scalar(
                    ap3(u_ext[_cu][:, :], 0, [[UE, HC], [1, 1]]),
                    ap3(u_ext[1 - _cu][:, :], T, [[UE, HC], [1, 1]]),
                    1.0, None, ALU.mult),
                    [cross["uext_prev_done"], cross.get(f"uext_free{cur}")])
            if c > 0:
                tre = S.add("DVE", lambda e, _cu=cur: e.tensor_scalar(
                    ap3(u_ext[1 - _cu][:, :], T + 1, [[UE, HC], [1, 1]]),
                    ap3(u_ext[_cu][:, :], 1, [[UE, HC], [1, 1]]),
                    1.0, None, ALU.mult), [ev_u[0], cross["uext_prev_done"]])
                conv_deps_of[c - 1] = [cross["uext_prev_done"], tre, cross[f"le_tok{1 - cur}"]]
            cross[f"le_tok{cur}"] = tle

        bstate = {}

        def stage_B_front(cc):
            t0 = cc * T
            g = cc % 2
            deps = conv_deps_of.pop(cc)
            tcv = S.add("DVE", lambda e, _g=g: e.tensor_tensor(
                ap3(ucb[:, :], 0, [[T, HC], [1, T]]),
                ap3(u_ext[_g][:, :], 0, [[UE, HC], [1, T]]),
                ap3(convw[:, :], 0, [[3, HC], [0, T]]), ALU.mult),
                deps + [tload["convw"], cross.get("uc_free")])
            tc_toks = []
            tprev = tcv
            for q in range(NG):
                c0 = q * GH * T
                tcv2 = S.add("DVE", lambda e, _g=g, _q=q: e.tensor_tensor(
                    ap3(tmpb[:, :], 0, [[T, GH], [1, T]]),
                    ap3(u_ext[_g][:, :], _q * GH * UE + 1, [[UE, GH], [1, T]]),
                    ap3(convw[:, :], _q * GH * 3 + 1, [[3, GH], [0, T]]), ALU.mult),
                    [tprev, cross.get("tmpb_free") if q == 0 else None])
                ta1 = S.add("DVE", lambda e, _c0=c0: e.tensor_tensor(
                    ucb[:, _c0:_c0 + GH * T], ucb[:, _c0:_c0 + GH * T],
                    tmpb[:, 0:GH * T], ALU.add), [tcv2])
                tcv3 = S.add("DVE", lambda e, _g=g, _q=q: e.tensor_tensor(
                    ap3(tmpb[:, :], 0, [[T, GH], [1, T]]),
                    ap3(u_ext[_g][:, :], _q * GH * UE + 2, [[UE, GH], [1, T]]),
                    ap3(convw[:, :], _q * GH * 3 + 2, [[3, GH], [0, T]]), ALU.mult), [ta1])
                ta2 = S.add("DVE", lambda e, _c0=c0: e.tensor_tensor(
                    ucb[:, _c0:_c0 + GH * T], ucb[:, _c0:_c0 + GH * T],
                    tmpb[:, 0:GH * T], ALU.add), [tcv3])
                tc_toks.append(ta2)
                tprev = ta2
            cross[f"uext_free{g}"] = tc_toks[-1]
            cross["tmpb_free"] = tc_toks[-1]
            tsil = []
            for hc in range(HC):
                tsil.append(S.add("ACT", lambda e, _hc=hc: e.activation(
                    ucb[:, _hc * T:(_hc + 1) * T], ucb[:, _hc * T:(_hc + 1) * T],
                    AF.Silu, bias=convb[:, _hc:_hc + 1]),
                    [tc_toks[hc // GH], tload["convb"]]))
            bk, bg = next_bank()
            mmx = None
            for hc in range(HC):
                mmx = S.add("PE", lambda e, _hc=hc, _b=bk: e.matmul(
                    pb[_b][0:96, :], wxpT[:, _hc * 96:(_hc + 1) * 96],
                    ucb[:, _hc * T:(_hc + 1) * T],
                    start=(_hc == 0), stop=(_hc == HC - 1)),
                    [tsil[hc], tload["wxpT"], bg if hc == 0 else None, mmx])
            par = cc % 2
            tdtp = S.add("ACT", lambda e, _p=par, _b=bk: e.copy(dtpb[_p][0:64, :], pb[_b][0:64, :]),
                         [mmx, cross.get(f"dtp_free{par}")])
            tbc = S.add("ACT", lambda e, _p=par, _b=bk: e.copy(bcb[_p][0:32, :], pb[_b][64:96, :]),
                        [mmx, cross.get(f"bcb_free{par}")])
            bank_guard[bk] = tbc
            _, dbc = S.dma("ACT", bc_h[0:32, t0:t0 + T], bcb[par][0:32, :], [tbc])
            cross[f"bcb_free{par}"] = dbc
            tdt = []
            for hc in range(HC):
                bk2, bg2 = next_bank()
                mmd = S.add("PE", lambda e, _hc=hc, _b=bk2, _p=par: e.matmul(
                    pb[_b][:, :], wdtT[0:64, _hc * 128:(_hc + 1) * 128], dtpb[_p][0:64, :],
                    start=True, stop=True),
                    [tdtp, tload["wdtT"], bg2])
                ep2 = hc % 2
                te0 = S.add("ACT", lambda e, _hc=hc, _b=bk2, _e2=ep2: e.activation(
                    ebuf[_e2][:, :], pb[_b][:, :], AF.Exp,
                    bias=bdt[:, _hc:_hc + 1]),
                    [mmd, tload["bdt"], cross.get(f"ebuf_free{ep2}")])
                bank_guard[bk2] = te0
                te1 = S.add("ACT", lambda e, _hc=hc, _e2=ep2: e.activation(
                    dtb[:, _hc * T:(_hc + 1) * T], ebuf[_e2][:, :], AF.Ln,
                    bias=1.0), [te0])
                cross[f"ebuf_free{ep2}"] = te1
                tdt.append(te1)
            cross[f"dtp_free{par}"] = tdt[-1]
            bstate[cc] = {"tsil": tsil, "tdt": tdt, "dbc": dbc}

        def stage_B_scan(cc):
            t0 = cc * T
            g = cc % 2
            par = cc % 2
            bs = bstate.pop(cc)
            tsil, tdt, dbc = bs["tsil"], bs["tdt"], bs["dbc"]
            ydeps = [cross.get("y_used")]
            for n in range(NS):
                npar = n % 2
                _, dbb = S.dma("SP", bbc[npar][:, :],
                               hap(bc_h, n * L + t0, [[0, 128], [0, GH], [1, T]]),
                               [cross.get(f"bbc_free{npar}"), dbc])
                _, dcb = S.dma("SP", cbc[npar][:, :],
                               hap(bc_h, (NS + n) * L + t0, [[0, 128], [0, GH], [1, T]]),
                               [cross.get(f"cbc_free{npar}"), dbc])
                for q in range(NG):
                    it = n * NG + q
                    spar = it % 2
                    col0 = n * HC + q * GH
                    tda = S.add("ACT", lambda e, _q=q, _p=spar, _n=n: e.activation(
                        dAb[_p][:, :], dtb[:, _q * GH * T:(_q + 1) * GH * T],
                        AF.Exp, scale=a_bc[:, _n:_n + 1]),
                        [tdt[q * GH + GH - 1] if n == 0 else None, t_abc,
                         cross.get(f"dA_free{spar}")])
                    tbu = S.add("DVE", lambda e, _q=q, _p=spar, _np=npar: e.tensor_tensor(
                        bub[_p][:, :], ucb[:, _q * GH * T:(_q + 1) * GH * T],
                        bbc[_np][:, :], ALU.mult),
                        [tsil[(q + 1) * GH - 1] if n == 0 else None, dbb,
                         cross.get(f"bu_free{spar}")])
                    tscan = []
                    for j in range(GH):
                        tscan.append(S.add("DVE", lambda e, _j=j, _p=spar, _c=col0: e.tensor_tensor_scan(
                            hsb[_p][:, _j * T:(_j + 1) * T],
                            dAb[_p][:, _j * T:(_j + 1) * T],
                            bub[_p][:, _j * T:(_j + 1) * T],
                            state[:, _c + _j:_c + _j + 1], ALU.mult, ALU.add),
                            [tda, tbu, tzs,
                             cross.get(f"carry{col0 + j}"),
                             cross.get(f"hs_free{spar}") if j == 0 else None]))
                    tcar = S.add("ACT", lambda e, _p=spar, _c=col0: e.copy(
                        state[:, _c:_c + GH],
                        ap3(hsb[_p][:, :], T - 1, [[T, GH], [1, 1]])), [tscan[-1]])
                    for j in range(GH):
                        cross[f"carry{col0 + j}"] = tcar
                    cross[f"dA_free{spar}"] = tscan[-1]
                    cross[f"bu_free{spar}"] = tscan[-1]
                    if n == 0:
                        tct = S.add("DVE", lambda e, _q=q, _p=spar, _np=npar: e.tensor_tensor(
                            y_acc[:, _q * GH * T:(_q + 1) * GH * T],
                            hsb[_p][:, :], cbc[_np][:, :], ALU.mult),
                            [tscan[-1], dcb] + (ydeps if q == 0 else []))
                        cross[f"hs_free{spar}"] = tct
                        tya = tct
                    else:
                        tct = S.add("DVE", lambda e, _p=spar, _np=npar: e.tensor_tensor(
                            tmpb[:, :], hsb[_p][:, :], cbc[_np][:, :], ALU.mult),
                            [tscan[-1], dcb, cross.get("tmpb_free")])
                        cross[f"hs_free{spar}"] = tct
                        tya = S.add("DVE", lambda e, _q=q: e.tensor_tensor(
                            y_acc[:, _q * GH * T:(_q + 1) * GH * T],
                            y_acc[:, _q * GH * T:(_q + 1) * GH * T],
                            tmpb[:, 0:GH * T], ALU.add), [tct, cross.get(f"ya{q}")])
                        cross["tmpb_free"] = tya
                    cross[f"ya{q}"] = tya
                cross[f"bbc_free{npar}"] = cross[f"bu_free{(n * NG + NG - 1) % 2}"]
                cross[f"cbc_free{npar}"] = cross[f"ya{NG - 1}"]
            ylast = [cross[f"ya{q}"] for q in range(NG)]
            yfin = []
            for q in range(NG):
                zpar = 0
                _, dzl = S.dma("SP", zld[zpar][:, :],
                               hap(z_h, q * GH * 128 * L + t0, [[L, 128], [128 * L, GH], [1, T]]),
                               [cross.get(f"zld_free{zpar}")] + cross[f"z_spilled{g}"])
                tzsil = S.add("ACT", lambda e, _z=zpar: e.activation(
                    zld[_z][:, :], zld[_z][:, :], AF.Silu), [dzl])
                tduc = []
                for j in range(GH):
                    hc = q * GH + j
                    tduc.append(S.add("DVE", lambda e, _hc=hc, _j=j: e.tensor_scalar(
                        tmpb[:, _j * T:(_j + 1) * T], ucb[:, _hc * T:(_hc + 1) * T],
                        dvec[:, _hc:_hc + 1], None, ALU.mult),
                        [tload["dvec"], cross.get("tmpb_free") if j == 0 else None]))
                ta = S.add("DVE", lambda e, _q=q: e.tensor_tensor(
                    y_acc[:, _q * GH * T:(_q + 1) * GH * T],
                    y_acc[:, _q * GH * T:(_q + 1) * GH * T],
                    tmpb[:, 0:GH * T], ALU.add), [tduc[-1], ylast[q]])
                cross["tmpb_free"] = ta
                tm2 = S.add("DVE", lambda e, _q=q, _z=zpar: e.tensor_tensor(
                    y_acc[:, _q * GH * T:(_q + 1) * GH * T],
                    y_acc[:, _q * GH * T:(_q + 1) * GH * T],
                    zld[_z][:, 0:GH * T], ALU.mult), [ta, tzsil])
                cross[f"zld_free{zpar}"] = tm2
                yfin.append(tm2)
            cross["uc_free"] = yfin[-1]
            tx2 = []
            for dc in range(DC):
                wpar = dc % 2
                _, dwo = S.dma("POOL", wob[wpar][:, :],
                               hap(woutT_h, dc * 128, [[D, 128], [128 * D, HC], [1, 128]]),
                               [cross.get(f"wob_free{wpar}")])
                epar = dc % 2
                _, dxr = S.dma("SP", xres[epar][:, :],
                               x_h[dc * 128:(dc + 1) * 128, t0:t0 + T],
                               [cross.get(f"xres_free{epar}")])
                bk3, bg3 = next_bank()
                mmt = None
                for kc in range(HC):
                    mmt = S.add("PE", lambda e, _dc=dc, _kc=kc, _w=wpar, _b=bk3: e.matmul(
                        pb[_b][:, :], wob[_w][:, _kc * 128:(_kc + 1) * 128],
                        y_acc[:, _kc * T:(_kc + 1) * T],
                        start=(_kc == 0), stop=(_kc == HC - 1)),
                        [yfin[kc // GH], dwo, bg3 if kc == 0 else None, mmt])
                cross[f"wob_free{wpar}"] = mmt
                te1 = S.add("DVE", lambda e, _dc=dc, _b=bk3, _e=epar, _g=g: e.tensor_tensor(
                    e1f[_e][:, :], pb[_b][:, :], h1c[_g][:, _dc * T:(_dc + 1) * T], ALU.add),
                    [mmt, cross.get(f"e1_free{epar}")])
                bank_guard[bk3] = te1
                tx = S.add("DVE", lambda e, _dc=dc, _e=epar: e.scalar_tensor_tensor(
                    x2b[:, _dc * T:(_dc + 1) * T], e1f[_e][:, :],
                    modc[:, 2 * DC + _dc:2 * DC + _dc + 1],
                    xres[_e][:, :], ALU.mult, ALU.add),
                    [te1, dxr, gfix_toks[0], cross.get("x2b_free")])
                cross[f"e1_free{epar}"] = tx
                cross[f"xres_free{epar}"] = tx
                tx2.append(tx)
            cross[f"h1_free{g}"] = tx2[-1]
            cross["y_used"] = tx2[-1]
            _, dx2 = S.dma("ACT", hap(x2_h, t0, [[L, 128], [128 * L, DC], [1, T]]),
                           x2b[:, :], [tx2[-1]])
            msum = None
            msq = None
            for dc in range(DC):
                tq = S.add("ACT", lambda e, _dc=dc: e.activation(
                    xsqf[:, :], x2b[:, _dc * T:(_dc + 1) * T], AF.Square),
                    [tx2[dc], msq, cross.get("xsqf_free")])
                msum = S.add("PE", lambda e, _dc=dc: e.matmul(
                    pb[6][0:1, :], onesf[:, :], x2b[:, _dc * T:(_dc + 1) * T],
                    start=(_dc == 0), stop=(_dc == DC - 1)),
                    [tx2[dc], t_onesf, cross.get("b6_free") if dc == 0 else None, msum])
                msq = S.add("PE", lambda e, _dc=dc: e.matmul(
                    pb[7][0:1, :], onesf[:, :], xsqf[:, :],
                    start=(_dc == 0), stop=(_dc == DC - 1)),
                    [tq, cross.get("b7_free") if dc == 0 else None, msq])
            cross["xsqf_free"] = msq
            tm = S.add("DVE", lambda e: e.tensor_scalar(
                stats[0:1, 0:T], pb[6][0:1, :], 1.0 / D, None, ALU.mult),
                [msum, cross.get("stats_free")])
            cross["b6_free"] = tm
            tmsq = S.add("ACT", lambda e: e.activation(
                stats[0:1, 2 * T:3 * T], stats[0:1, 0:T], AF.Square), [tm])
            tv = S.add("DVE", lambda e: e.scalar_tensor_tensor(
                stats[0:1, T:2 * T], pb[7][0:1, :], 1.0 / D,
                stats[0:1, 2 * T:3 * T], ALU.mult, ALU.subtract), [msq, tmsq])
            cross["b7_free"] = tv
            tl = S.add("ACT", lambda e: e.activation(
                stats[0:1, 2 * T:3 * T], stats[0:1, T:2 * T], AF.Ln,
                bias=cst1[0:1, 0:1]), [tv, t_cst])
            tr = S.add("ACT", lambda e: e.activation(
                stats[0:1, 2 * T:3 * T], stats[0:1, 2 * T:3 * T], AF.Exp,
                scale=cst1[0:1, 1:2]), [tl, t_cst])
            _, dm1 = S.dma("ACT", st2_h[0:1, t0:t0 + T], stats[0:1, 0:T], [tm])
            _, dm2 = S.dma("ACT", st2_h[0:1, L + t0:L + t0 + T], stats[0:1, 2 * T:3 * T], [tr])
            cross["stats_free"] = [dm1, dm2]
            _, db1 = S.dma("SP", m2bc[:, 0:T], rep128(hap(st2_h, t0, [[1, T]])),
                           [dm1, cross.get("m2bc_free")])
            _, db2 = S.dma("SP", m2bc[:, T:2 * T], rep128(hap(st2_h, L + t0, [[1, T]])),
                           [dm2, cross.get("m2bc_free")])
            m2toks = []
            for dc in range(DC):
                epar = dc % 2
                ts1 = S.add("DVE", lambda e, _dc=dc, _e=epar: e.tensor_tensor(
                    h2sl[_e][:, :], x2b[:, _dc * T:(_dc + 1) * T],
                    m2bc[:, 0:T], ALU.subtract),
                    [db1, cross.get(f"h2sl_free{epar}")])
                ts2 = S.add("DVE", lambda e, _e=epar: e.tensor_tensor(
                    h2sl[_e][:, :], h2sl[_e][:, :], m2bc[:, T:2 * T], ALU.mult), [ts1, db2])
                ts3 = S.add("DVE", lambda e, _dc=dc, _e=epar: e.tensor_scalar(
                    h2sl[_e][:, :], h2sl[_e][:, :],
                    modc[:, 4 * DC + _dc:4 * DC + _dc + 1],
                    modc[:, 3 * DC + _dc:3 * DC + _dc + 1], ALU.mult, ALU.add),
                    [ts2, t_cb])
                _, dh2 = S.dma("ACT", h2_h[dc * 128:(dc + 1) * 128, t0:t0 + T],
                               h2sl[epar][:, :], [ts3])
                trm = S.add("DVE", lambda e, _e=epar: e.reduce_max(
                    e1f[_e][:, 0:1], h2sl[_e][:, :], axis=AX.X, apply_absolute_value=True),
                    [ts3, cross.get(f"e1_free{epar}")])
                trm2 = S.add("DVE", lambda e, _e=epar: e.tensor_tensor(
                    rmaxb[:, :], rmaxb[:, :], e1f[_e][:, 0:1], ALU.max),
                    [trm, tzr, cross.get("rmax_ser")])
                cross["rmax_ser"] = trm2
                cross[f"e1_free{epar}"] = trm2
                cross[f"h2sl_free{epar}"] = [dh2, trm]
                m2toks.append(trm2)
            cross["m2bc_free"] = m2toks[-1]
            cross["x2b_free"] = [m2toks[-1], dx2]
            cross["rmax"] = m2toks[-1]

        chunk_state = {}
        for c in range(NCH + 1):
            if c < NCH:
                stage_A(c)
                stage_xz(c, 0, 1)
                stage_edges(c)
            if c >= 1:
                cc = c - 1
                if cc == NCH - 1:
                    lastg = cc % 2
                    tre = S.add("DVE", lambda e, _g=lastg: e.memset(
                        ap3(u_ext[_g][:, :], T + 1, [[UE, HC], [1, 1]]), 0.0),
                        [cross["uext_prev_done"]])
                    conv_deps_of[cc] = [cross["uext_prev_done"], tre, cross[f"le_tok{lastg}"]]
                stage_B_front(cc)
            if c < NCH:
                stage_xz(c, 1, ECN)
                cross["uext_prev_done"] = cross.pop("uext_prev_done_full")
            if c >= 1:
                stage_B_scan(c - 1)

        mix_ring = [S.ring_token("ACT"), S.ring_token("SP")]
        if stop_after == "MIX":
            S.add("SP", lambda e: e.nop(nofuse=True), mix_ring + [S.ring_token("POOL")])
            S.emit(nc, ctx)
            return nc
        S.barrier()
        pm.close()

        def allgather_scalar(rmax_sb, cc_in, cc_out, slot_gmax, dep):
            _, d0 = S.dma("ACT", hap(sc_h, 64, [[1, 128]]), rmax_sb[:, :], [dep])
            _, d1 = S.dma("SP", stats0[0:1, 0:128], hap(sc_h, 64, [[1, 128]]), [d0])
            t2 = S.add("DVE", lambda e: e.reduce_max(
                small[0:1, slot_gmax:slot_gmax + 1], stats0[0:1, 0:128], axis=AX.X), [d1])
            _, d2 = S.dma("ACT", cc_in, small[0:1, slot_gmax:slot_gmax + 1], [t2])
            t3 = S.add("POOL", lambda e: e.collective_compute(
                "AllGather", ALU.bypass, replica_groups=[list(range(NB))],
                ins=[cc_in], outs=[cc_out]), [d2])
            _, d3 = S.dma("SP", small[0:1, SL_CC:SL_CC + 8], cc_out, [t3])
            t4 = S.add("DVE", lambda e: e.reduce_max(
                small[0:1, slot_gmax:slot_gmax + 1], small[0:1, SL_CC:SL_CC + 8], axis=AX.X), [d3])
            return t4

        pf = ctx.enter_context(ExitStack())
        stats0 = SB(pf, "stats0", [1, 128])
        xq = SB(pf, "xq", [128, DC * L], BF16)
        wqg = SB(pf, "wqg", [128, DC * NI], BF16)
        wqu = SB(pf, "wqu", [128, DC * NI], BF16)
        sgb = [SB(pf, f"sgb{i}", [128, T]) for i in range(2)]
        usb = [SB(pf, f"usb{i}", [128, T]) for i in range(2)]
        pbuf = [SB(pf, f"pbuf{i}", [128, T]) for i in range(2)]
        pfq = pf.enter_context(ExitStack())
        h2r = [SB(pfq, f"h2r{i}", [128, 4 * T]) for i in range(2)]
        wldf = [SB(pfq, f"wldf{i}", [128, NI // 2]) for i in range(2)]
        rsrep = SB(pfq, "rsrep", [128, NI // 2])

        tg1 = allgather_scalar(rmaxb, cc_in1, cc_out1, SL_GM, cross["rmax"])
        ts1 = S.add("ACT", lambda e: e.activation(small[0:1, SL_T0:SL_T0 + 1],
                                                  small[0:1, SL_CX:SL_CX + 1], AF.Sigmoid), [t_cx])
        ts2 = S.add("DVE", lambda e: e.tensor_tensor(
            small[0:1, SL_SX:SL_SX + 1], small[0:1, SL_GM:SL_GM + 1],
            small[0:1, SL_T0:SL_T0 + 1], ALU.mult), [tg1, ts1])
        ts3 = S.add("DVE", lambda e: e.tensor_scalar(
            small[0:1, SL_SX:SL_SX + 1], small[0:1, SL_SX:SL_SX + 1], 1.0 / 127.0, None, ALU.mult), [ts2])
        ts4 = S.add("DVE", lambda e: e.reciprocal(small[0:1, SL_RX:SL_RX + 1],
                                                  small[0:1, SL_SX:SL_SX + 1]), [ts3])
        ts6 = S.add("DVE", lambda e: e.tensor_scalar(
            small[0:1, SL_SX127:SL_SX127 + 1], small[0:1, SL_SX:SL_SX + 1], 1.0 / 127.0, None, ALU.mult), [ts3])
        _, dsx = S.dma("ACT", sc_h[0:1, 16:18],
                       ap3(small[0:1, :], SL_RX, [[SL_SX127 - SL_RX, 2], [1, 1]]), [ts4, ts6],
                       allow_slow_non_contiguous=True)
        _, dsxr = S.dma("SP", sxr[:, 0:2], rep128(sc_h[0:1, 16:18]), [dsx])

        txq = {}
        for ci in range(2 * NCH):
            c, half = ci // 2, ci % 2
            t0 = c * T
            par = ci % 2
            _, dh2r = S.dma("SP", h2r[par][:, :],
                            hap(h2_h, half * 4 * 128 * L + t0, [[L, 128], [128 * L, 4], [1, T]]),
                            [txq.get(par)])
            tq1 = S.add("DVE", lambda e, _p=par: e.tensor_scalar(
                h2r[_p][:, :], h2r[_p][:, :], sxr[:, 0:1], MAGIC, ALU.mult, ALU.add), [dh2r, dsxr])
            tq2 = S.add("DVE", lambda e, _p=par: e.tensor_scalar(
                h2r[_p][:, :], h2r[_p][:, :], MAGIC, -128.0, ALU.subtract, ALU.max), [tq1])
            tq3 = S.add("DVE", lambda e, _t0=t0, _hf=half, _p=par: e.tensor_scalar(
                ap3(xq[:, :], _hf * 4 * L + _t0, [[L, 4], [1, T]]), h2r[_p][:, :],
                127.0, None, ALU.min), [tq2])
            txq[par] = tq3
        txq_all = [txq[0], txq[1]]

        NH = NI // 2
        wq_toks = {"g": [[None, None] for _ in range(DC)],
                   "u": [[None, None] for _ in range(DC)]}
        drs_prev = None
        for nm, wTh, rs_h_, wq_sb, dbounce in (("g", wgT_h, rsg_h, wqg, rs_bounced[0]),
                                               ("u", wuT_h, rsu_h, wqu, rs_bounced[1])):
            for hf in range(2):
                _, drs = S.dma("SP", rsrep[:, :], rep128(hap(rs_h_, hf * NH, [[1, NH]])),
                               [dbounce, drs_prev])
                last_q = None
                for dc in range(DC):
                    par = dc % 2
                    _, dwt = S.dma("POOL", wldf[par][:, :],
                                   wTh[dc * 128:(dc + 1) * 128, hf * NH:(hf + 1) * NH],
                                   [cross.get(f"wldf_free{par}")])
                    tq1 = S.add("DVE", lambda e, _p=par: e.tensor_tensor(
                        wldf[_p][:, :], wldf[_p][:, :], rsrep[:, :],
                        ALU.mult), [dwt, drs])
                    tq2 = S.add("DVE", lambda e, _p=par: e.tensor_scalar(
                        wldf[_p][:, :], wldf[_p][:, :], MAGIC, MAGIC, ALU.add, ALU.subtract), [tq1])
                    tq3 = S.add("DVE", lambda e, _p=par, _dc=dc, _hf=hf, _w=wq_sb: e.tensor_scalar(
                        _w[:, _dc * NI + _hf * NH:_dc * NI + (_hf + 1) * NH],
                        wldf[_p][:, :], 1.0, None, ALU.mult), [tq2])
                    cross[f"wldf_free{par}"] = tq3
                    wq_toks[nm][dc][hf] = tq3
                    last_q = tq3
                drs_prev = last_q
        tscg = S.add("DVE", lambda e: e.tensor_scalar(
            wsceg[:, :], wmaxg[:, :], sxr[:, 1:2], None, ALU.mult), [dsxr, trg, rs_bounced[0]])
        tscu = S.add("DVE", lambda e: e.tensor_scalar(
            wsceu[:, :], wmaxu[:, :], sxr[:, 1:2], None, ALU.mult), [dsxr, tru, rs_bounced[1]])

        pfq.close()
        tzp = S.add("DVE", lambda e: e.memset(rpx[:, :], 0.0))
        l5 = {}
        for ic in range(ICN):
            for half in range(2):
                tb0 = half * 4
                mg = [None] * 4
                mu = [None] * 4
                for kc in range(DC):
                    for j in range(4):
                        tb = tb0 + j
                        mg[j] = S.add("PE", lambda e, _ic=ic, _kc=kc, _b=j, _tb=tb: e.matmul(
                            pb[_b][:, :], wqg[:, _kc * NI + _ic * 128:_kc * NI + _ic * 128 + 128],
                            xq[:, _kc * L + _tb * T:_kc * L + _tb * T + T],
                            start=(_kc == 0), stop=(_kc == DC - 1)),
                            [wq_toks["g"][kc],
                             txq_all if kc == 0 and j == 0 else None,
                             l5.get(f"pb{j}_free") if kc == 0 else None, mg[j]])
                    for j in range(4):
                        tb = tb0 + j
                        mu[j] = S.add("PE", lambda e, _ic=ic, _kc=kc, _b=4 + j, _tb=tb: e.matmul(
                            pb[_b][:, :], wqu[:, _kc * NI + _ic * 128:_kc * NI + _ic * 128 + 128],
                            xq[:, _kc * L + _tb * T:_kc * L + _tb * T + T],
                            start=(_kc == 0), stop=(_kc == DC - 1)),
                            [wq_toks["u"][kc],
                             l5.get(f"pb{4 + j}_free") if kc == 0 else None, mu[j]])
                for j in range(4):
                    tb = tb0 + j
                    ep = j % 2
                    tsg = S.add("ACT", lambda e, _b=j, _e=ep, _ic=ic: e.activation(
                        sgb[_e][:, :], pb[_b][:, :], AF.Silu, scale=wsceg[:, _ic:_ic + 1]),
                        [mg[j], tscg, l5.get(f"sgb_free{ep}")])
                    l5[f"pb{j}_free"] = tsg
                    tus = S.add("ACT", lambda e, _b=4 + j, _e=ep, _ic=ic: e.activation(
                        usb[_e][:, :], pb[_b][:, :], AF.Copy, scale=wsceu[:, _ic:_ic + 1]),
                        [mu[j], tscu, l5.get(f"usb_free{ep}")])
                    l5[f"pb{4 + j}_free"] = tus
                    tp = S.add("DVE", lambda e, _e=ep: e.tensor_tensor(
                        pbuf[_e][:, :], sgb[_e][:, :], usb[_e][:, :], ALU.mult),
                        [tsg, tus, l5.get(f"pbuf_free{ep}")])
                    l5[f"sgb_free{ep}"] = tp
                    l5[f"usb_free{ep}"] = tp
                    trp = S.add("DVE", lambda e, _e=ep: e.reduce_max(
                        sgb[_e][:, 0:1], pbuf[_e][:, :], axis=AX.X, apply_absolute_value=True), [tp])
                    trp2 = S.add("DVE", lambda e, _e=ep: e.tensor_tensor(
                        rpx[:, :], rpx[:, :], sgb[_e][:, 0:1], ALU.max),
                        [trp, tzp, l5.get("rpx_ser")])
                    l5["rpx_ser"] = trp2
                    _, dps = S.dma("ACT", p_h[ic * 128:(ic + 1) * 128, tb * T:tb * T + T],
                                   pbuf[ep][:, :], [tp])
                    l5[f"pbuf_free{ep}"] = dps
        pmax_dep = [l5["rpx_ser"], S.ring_token("ACT")]

        tg2 = allgather_scalar(rpx, cc_in2, cc_out2, SL_GM2, pmax_dep)
        tq1 = S.add("ACT", lambda e: e.activation(small[0:1, SL_T1:SL_T1 + 1],
                                                  small[0:1, SL_CD:SL_CD + 1], AF.Sigmoid), [t_cd])
        tq2 = S.add("DVE", lambda e: e.tensor_tensor(
            small[0:1, SL_SD:SL_SD + 1], small[0:1, SL_GM2:SL_GM2 + 1],
            small[0:1, SL_T1:SL_T1 + 1], ALU.mult), [tg2, tq1])
        tq3 = S.add("DVE", lambda e: e.tensor_scalar(
            small[0:1, SL_SD:SL_SD + 1], small[0:1, SL_SD:SL_SD + 1], 1.0 / 127.0, None, ALU.mult), [tq2])
        tq4 = S.add("DVE", lambda e: e.reciprocal(small[0:1, SL_RD:SL_RD + 1],
                                                  small[0:1, SL_SD:SL_SD + 1]), [tq3])
        tq6 = S.add("DVE", lambda e: e.tensor_scalar(
            small[0:1, SL_SD127:SL_SD127 + 1], small[0:1, SL_SD:SL_SD + 1], 1.0 / 127.0, None, ALU.mult), [tq3])
        _, dsd = S.dma("ACT", sc_h[0:1, 20:22],
                       ap3(small[0:1, :], SL_RD, [[SL_SD127 - SL_RD, 2], [1, 1]]), [tq4, tq6],
                       allow_slow_non_contiguous=True)
        _, dsdr = S.dma("SP", sxr[:, 2:4], rep128(sc_h[0:1, 20:22]), [dsd])

        ffn_ring = [S.ring_token("ACT"), S.ring_token("SP"), S.ring_token("POOL")]
        if stop_after == "P5":
            S.add("SP", lambda e: e.nop(nofuse=True), ffn_ring)
            S.emit(nc, ctx)
            return nc
        S.barrier()
        pf.close()

        p7 = ctx.enter_context(ExitStack())
        wqd = SB(p7, "wqd", [128, ICN * D], BF16)
        wld7 = [SB(p7, f"wld7{i}", [128, D]) for i in range(2)]
        rsrep7 = SB(p7, "rsrep7", [128, D])
        hq = [SB(p7, f"hq{i}", [128, ICN * T], BF16) for i in range(2)]
        pld = [SB(p7, f"pld{i}", [128, 4 * T]) for i in range(2)]
        x2rf = SB(p7, "x2rf", [128, DC * T])
        ob = [SB(p7, f"ob{i}", [128, T]) for i in range(2)]

        l7 = {}
        _, drs7 = S.dma("SP", rsrep7[:, :], rep128(hap(rsd_h, 0, [[1, D]])), [rs_bounced[2]])
        wd_toks = []
        for ic in range(ICN):
            par = ic % 2
            _, dwt = S.dma("POOL", wld7[par][:, :], wdT_h[ic * 128:(ic + 1) * 128, :],
                           [l7.get(f"wld7_free{par}")])
            tq1 = S.add("DVE", lambda e, _p=par: e.tensor_tensor(
                wld7[_p][:, :], wld7[_p][:, :], rsrep7[:, :], ALU.mult), [dwt, drs7])
            tq2 = S.add("DVE", lambda e, _p=par: e.tensor_scalar(
                wld7[_p][:, :], wld7[_p][:, :], MAGIC, MAGIC, ALU.add, ALU.subtract), [tq1])
            tq3 = S.add("DVE", lambda e, _p=par, _ic=ic: e.tensor_scalar(
                wqd[:, _ic * D:(_ic + 1) * D], wld7[_p][:, :], 1.0, None, ALU.mult), [tq2])
            l7[f"wld7_free{par}"] = tq3
            wd_toks.append(tq3)
        tcs = []
        for dc in range(DC):
            tc1 = S.add("DVE", lambda e, _d=dc: e.tensor_scalar(
                csc[:, _d:_d + 1], wmaxd[:, _d:_d + 1], sxr[:, 3:4], None, ALU.mult),
                [dsdr, trd, rs_bounced[2], tcs[-1] if tcs else None])
            tc2 = S.add("DVE", lambda e, _d=dc: e.tensor_tensor(
                csc[:, _d:_d + 1], csc[:, _d:_d + 1],
                modc[:, 5 * DC + _d:5 * DC + _d + 1], ALU.mult), [tc1, gfix_toks[1]])
            tcs.append(tc2)

        final_outs = []
        for tb in range(NCH):
            t0 = tb * T
            hq_toks = []
            for icg in range((ICN + 3) // 4):
                i0 = icg * 4
                ni_ = min(4, ICN - i0)
                par = icg % 2
                _, dpl = S.dma("POOL", pld[par][:, 0:ni_ * T],
                               hap(p_h, i0 * 128 * L + t0, [[L, 128], [128 * L, ni_], [1, T]]),
                               [l7.get(f"pld_free{par}")])
                th1 = S.add("DVE", lambda e, _p=par, _n=ni_: e.tensor_scalar(
                    pld[_p][:, 0:_n * T], pld[_p][:, 0:_n * T], sxr[:, 2:3], MAGIC,
                    ALU.mult, ALU.add), [dpl, dsdr])
                th2 = S.add("DVE", lambda e, _p=par, _n=ni_: e.tensor_scalar(
                    pld[_p][:, 0:_n * T], pld[_p][:, 0:_n * T], MAGIC, -128.0,
                    ALU.subtract, ALU.max), [th1])
                th3 = S.add("DVE", lambda e, _p=par, _n=ni_, _i0=i0, _hp=tb % 2: e.tensor_scalar(
                    hq[_hp][:, _i0 * T:(_i0 + _n) * T], pld[_p][:, 0:_n * T],
                    127.0, None, ALU.min), [th2, l7.get(f"hq_free{tb % 2}")])
                l7[f"pld_free{par}"] = th3
                hq_toks.append(th3)
            dx2rs = []
            for dc in range(DC):
                _, dx2r = S.dma("SP", x2rf[:, dc * T:(dc + 1) * T],
                                x2_h[dc * 128:(dc + 1) * 128, t0:t0 + T],
                                [l7.get("x2rf_free") if dc == 0 else None])
                dx2rs.append(dx2r)
            mm = [None] * DC
            for kc in range(ICN):
                for dc in range(DC):
                    mm[dc] = S.add("PE", lambda e, _dc=dc, _kc=kc, _hp=tb % 2: e.matmul(
                        pb[_dc][:, :], wqd[:, _kc * D + _dc * 128:_kc * D + _dc * 128 + 128],
                        hq[_hp][:, _kc * T:(_kc + 1) * T],
                        start=(_kc == 0), stop=(_kc == ICN - 1)),
                        [hq_toks[kc // 4] if dc == 0 else None, wd_toks[kc] if dc == 0 else None,
                         l7.get(f"pb{dc}_free") if kc == 0 else None, mm[dc]])
            for dc in range(DC):
                par = dc % 2
                tout = S.add("DVE", lambda e, _dc=dc, _p=par: e.scalar_tensor_tensor(
                    ob[_p][:, :], pb[_dc][:, :], csc[:, _dc:_dc + 1],
                    x2rf[:, _dc * T:(_dc + 1) * T],
                    ALU.mult, ALU.add), [mm[dc], dx2rs[dc], tcs[dc], l7.get(f"ob_free{par}")])
                l7[f"pb{dc}_free"] = tout
                _, dout = S.dma("ACT", out_h[dc * 128:(dc + 1) * 128, t0:t0 + T],
                                ob[par][:, :], [tout])
                l7[f"ob_free{par}"] = dout
                final_outs.append(dout)
            l7["x2rf_free"] = [S.last_item["DVE"]]
            l7[f"hq_free{tb % 2}"] = mm[-1]
        S.add("SP", lambda e: e.nop(nofuse=True), final_outs)
        p7.close()

        S.emit(nc, ctx)
    return nc


_NC_CACHE = {}


def kernel(**inputs):
    bf = ml_dtypes.bfloat16
    x = np.asarray(inputs["L0_attn_x"], dtype=np.float32)
    conv_w = np.asarray(inputs["L0_attn_conv_w"], dtype=np.float32)[:, 0, :]
    fm = lambda v, nch: np.ascontiguousarray(np.asarray(v, np.float32).reshape(nch, 128).T)

    w_in = np.asarray(inputs["L0_attn_W_in"], np.float32)
    wxp = np.asarray(inputs["L0_attn_W_xproj"], np.float32)
    wdt = np.asarray(inputs["L0_attn_W_dt"], np.float32)
    wout = np.asarray(inputs["L0_attn_W_out"], np.float32)
    wg = np.asarray(inputs["L0_ffn_w_gate"], np.float32)
    wu = np.asarray(inputs["L0_ffn_w_up"], np.float32)
    wd = np.asarray(inputs["L0_ffn_w_down"], np.float32)

    common = {
        "w_inT": np.ascontiguousarray(w_in.T),
        "convw": np.ascontiguousarray(
            conv_w.reshape(HC, 128, 3).transpose(1, 0, 2).reshape(128, HC * 3).astype(bf)),
        "convb": fm(inputs["L0_attn_conv_b"], HC),
        "wxpT": np.ascontiguousarray(wxp.T),
        "wdtT": np.ascontiguousarray(wdt.T),
        "bdt": fm(inputs["L0_attn_b_dt"], HC),
        "alog": np.asarray(inputs["L0_attn_A_log"], np.float32).reshape(1, NS),
        "dvec": fm(inputs["L0_attn_D"], HC),
        "woutT": np.ascontiguousarray(wout.T),
        "clipx": np.asarray(inputs["L0_ffn_clip_x"], np.float32).reshape(1, 1),
        "clipd": np.asarray(inputs["L0_ffn_clip_down"], np.float32).reshape(1, 1),
        "wg": np.ascontiguousarray(wg),
        "wu": np.ascontiguousarray(wu),
        "wd": np.ascontiguousarray(wd),
        "wgT": np.ascontiguousarray(wg.T),
        "wuT": np.ascontiguousarray(wu.T),
        "wdT": np.ascontiguousarray(wd.T),
        "wmod": np.ascontiguousarray(np.asarray(inputs["L0_wmod"], np.float32)),
        "onesb": np.ones((128, 1), dtype=bf),
    }
    cmat = np.asarray(inputs["L0_c"], np.float32)
    in_maps = []
    for b in range(NB):
        m = dict(common)
        m["x"] = np.ascontiguousarray(x[b].T)
        m["cvec"] = fm(cmat[b], DC)
        in_maps.append(m)

    if "nc" not in _NC_CACHE:
        _NC_CACHE["nc"] = build()
    nc = _NC_CACHE["nc"]
    res = run_bass_kernel_spmd(nc, in_maps, core_ids=list(range(NB)))
    out = np.empty((NB, L, D), dtype=np.float32)
    for b in range(NB):
        out[b] = np.asarray(res.results[b]["out"], dtype=np.float32).T
    return out


# version 07
# speedup vs baseline: 1.1178x; 1.1178x over previous
import numpy as np
import ml_dtypes
from contextlib import ExitStack

import concourse.bass as bass
import concourse.mybir as mybir
from concourse.bass_utils import run_bass_kernel_spmd

F32 = mybir.dt.float32
BF16 = mybir.dt.bfloat16
AF = mybir.ActivationFunctionType
ALU = mybir.AluOpType
AX = mybir.AxisListType

MAGIC = 12582912.0

L = 4096
D = 1024
H = 2048
E = 2 * H
NS = 16
DR = 64
NI = 2816
NB = 8

T = 512
NCH = L // T
DC = D // 128
HC = H // 128
ECN = E // 128
ICN = NI // 128
UE = T + 2
NG = 8
GH = HC // NG

ENGS = ("PE", "ACT", "DVE", "POOL", "SP")


class Builder:
    def __init__(self):
        self.items = []
        self.dma_counts = {"SP": 0, "ACT": 0, "POOL": 0}
        self.pending_barrier = {e: None for e in ENGS}
        self.last_item = {e: None for e in ENGS}

    @staticmethod
    def _flat(deps):
        out = []
        for d in deps:
            if d is None:
                continue
            if isinstance(d, list):
                out.extend(x for x in d if x is not None)
            else:
                out.append(d)
        return out

    def ring_token(self, ring):
        return ("D", ring, self.dma_counts[ring])

    def add(self, eng, fn, deps=()):
        deps = self._flat(deps)
        if self.pending_barrier[eng]:
            deps.extend(self.pending_barrier[eng])
            self.pending_barrier[eng] = None
        idx = len(self.items)
        for d in deps:
            assert d[0] in ("i", "d", "D"), d
            assert d[0] != "i" or d[1] < idx
        self.items.append([eng, deps, ("x", fn), False])
        tok = ("i", idx, eng)
        self.last_item[eng] = tok
        return tok

    def dma(self, eng, out, in_, deps=(), **kw):
        deps = self._flat(deps)
        if self.pending_barrier[eng]:
            deps.extend(self.pending_barrier[eng])
            self.pending_barrier[eng] = None
        ring = eng
        self.dma_counts[ring] += 1
        cnt = self.dma_counts[ring]
        idx = len(self.items)

        def fn(e, _o=out, _i=in_, _kw=kw):
            return e.dma_start(out=_o, in_=_i, **_kw)
        self.items.append([eng, deps, ("dma", fn, ring), False])
        tok = ("i", idx, eng)
        self.last_item[eng] = tok
        return tok, ("d", ring, cnt)

    def barrier(self):
        rings = [self.ring_token(r) for r in ("SP", "ACT", "POOL")]
        toks = [self.add(e, None, [self.last_item[e]] + rings) for e in ENGS]
        for e in ENGS:
            self.pending_barrier[e] = [t for t in toks if t[2] != e]

    def emit(self, nc, ctx):
        n = len(self.items)
        for eng, deps, fn, _ in self.items:
            for d in deps:
                if d[0] == "i":
                    self.items[d[1]][3] = True
        inc_after = [0] * n
        counters = {e: 0 for e in ENGS}
        for i, (eng, deps, fn, needs) in enumerate(self.items):
            if needs:
                counters[eng] += 1
            inc_after[i] = counters[eng]

        NSEM = 16
        prog = {e: ctx.enter_context(nc.semaphore(f"prog{e}")) for e in ENGS}
        dsem = {r: [ctx.enter_context(nc.semaphore(f"dma{r}{k}")) for k in range(NSEM)]
                for r in ("SP", "ACT", "POOL")}
        items = self.items
        dma_index = {}
        nxt = {r: 0 for r in ("SP", "ACT", "POOL")}
        for i, (eng, deps, fn, needs) in enumerate(items):
            if fn[0] == "dma":
                dma_index[i] = nxt[fn[2]]
                nxt[fn[2]] += 1

        def dtok_semval(ring, cnt):
            j = cnt - 1
            return dsem[ring][j % NSEM], 16 * (j // NSEM + 1)

        def run_engine(eobj, eng):
            last_wait = {}
            for i, (e2, deps, fn, needs) in enumerate(items):
                if e2 != eng:
                    continue
                for d in deps:
                    if d[0] == "i":
                        j = d[1]
                        pe = items[j][0]
                        sw = [(prog[pe], inc_after[j])]
                    elif d[0] == "d":
                        sw = [dtok_semval(d[1], d[2])]
                    else:
                        cnt = d[2]
                        sw = []
                        for k in range(min(cnt, NSEM)):
                            nk = (cnt - 1 - k) // NSEM + 1
                            sw.append((dsem[d[1]][k], 16 * nk))
                    for sem, val in sw:
                        if last_wait.get(sem.name, -1) >= val:
                            continue
                        eobj.wait_ge(sem, val)
                        last_wait[sem.name] = val
                if fn[0] == "dma":
                    j = dma_index[i]
                    sem = dsem[fn[2]][j % NSEM]
                    prev_use = j // NSEM
                    if prev_use > 0 and last_wait.get(sem.name, -1) < 16 * prev_use:
                        eobj.wait_ge(sem, 16 * prev_use)
                        last_wait[sem.name] = 16 * prev_use
                    ins = fn[1](eobj)
                    ins.then_inc(sem, 16)
                    if needs:
                        eobj.nop(nofuse=True).then_inc(prog[eng], 1)
                else:
                    f = fn[1]
                    ins = eobj.nop(nofuse=True) if f is None else f(eobj)
                    if needs:
                        ins.then_inc(prog[eng], 1)

        block = ctx.enter_context(nc.Block())
        block.tensor(lambda e: run_engine(e, "PE"))
        block.scalar(lambda e: run_engine(e, "ACT"))
        block.vector(lambda e: run_engine(e, "DVE"))
        block.gpsimd(lambda e: run_engine(e, "POOL"))
        block.sync(lambda e: run_engine(e, "SP"))


def rep128(src_ap, count=128):
    return bass.AP(src_ap.tensor, src_ap.offset,
                   [[0, count]] + [list(p) for p in src_ap.ap])


def ap3(full_ap, off, dims):
    return bass.AP(full_ap.tensor, full_ap.offset + off,
                   [list(full_ap.ap[0])] + [list(d) for d in dims])


def hap(full_ap, off, dims):
    return bass.AP(full_ap.tensor, full_ap.offset + off, [list(d) for d in dims])


def build(stop_after=None):
    nc = bass.Bass("TRN2", target_bir_lowering=False, debug=False)
    S = Builder()

    def hin(name, shape, dtype=F32):
        return nc.dram_tensor(name, shape, dtype, kind="ExternalInput").ap()

    x_h = hin("x", [D, L])
    w_inT_h = hin("w_inT", [D, E])
    convw_h = hin("convw", [128, HC * 3], BF16)
    convb_h = hin("convb", [128, HC])
    wxpT_h = hin("wxpT", [H, 96])
    wdtT_h = hin("wdtT", [DR, H])
    bdt_h = hin("bdt", [128, HC])
    alog_h = hin("alog", [1, NS])
    dvec_h = hin("dvec", [128, HC])
    woutT_h = hin("woutT", [H, D])
    clipx_h = hin("clipx", [1, 1])
    clipd_h = hin("clipd", [1, 1])
    wg_h = hin("wg", [NI, D])
    wu_h = hin("wu", [NI, D])
    wd_h = hin("wd", [D, NI])
    wgT_h = hin("wgT", [D, NI])
    wuT_h = hin("wuT", [D, NI])
    wdT_h = hin("wdT", [NI, D])
    cvec_h = hin("cvec", [128, DC])
    wmod_h = hin("wmod", [D, 6 * D])
    onesb_h = hin("onesb", [128, 1], BF16)

    out_h = nc.dram_tensor("out", [D, L], F32, kind="ExternalOutput").ap()

    bc_h = nc.dram_tensor("bc_s", [2 * NS, L], BF16, kind="Internal").ap()
    z_h = nc.dram_tensor("z_s", [H, L], BF16, kind="Internal").ap()
    x2_h = nc.dram_tensor("x2_s", [D, L], F32, kind="Internal").ap()
    h2_h = nc.dram_tensor("h2_s", [D, L], F32, kind="Internal").ap()
    p_h = nc.dram_tensor("p_s", [NI, L], F32, kind="Internal").ap()
    st1_h = nc.dram_tensor("st1_s", [1, 2 * L], BF16, kind="Internal").ap()
    st2_h = nc.dram_tensor("st2_s", [1, 2 * L], F32, kind="Internal").ap()
    mod_h = nc.dram_tensor("mod_s", [1, 6 * D], F32, kind="Internal").ap()
    sc_h = nc.dram_tensor("sc_s", [1, 256], F32, kind="Internal").ap()
    rsg_h = nc.dram_tensor("rsg_s", [1, NI], F32, kind="Internal").ap()
    rsu_h = nc.dram_tensor("rsu_s", [1, NI], F32, kind="Internal").ap()
    rsd_h = nc.dram_tensor("rsd_s", [1, D], F32, kind="Internal").ap()
    cc_in1 = nc.dram_tensor("cc_in1", [1, 1], F32, kind="Internal").ap()
    cc_out1 = nc.dram_tensor("cc_out1", [1, 8], F32, kind="Internal").ap()
    cc_in2 = nc.dram_tensor("cc_in2", [1, 1], F32, kind="Internal").ap()
    cc_out2 = nc.dram_tensor("cc_out2", [1, 8], F32, kind="Internal").ap()

    with ExitStack() as ctx:
        def SB(scope, name, shape, dt=F32):
            return scope.enter_context(nc.sbuf_tensor(name, shape, dt))

        pb = [ctx.enter_context(nc.psum_tensor(f"pb{i}", [128, 512], F32)) for i in range(8)]

        onesb = SB(ctx, "onesb_s", [128, 1], BF16)
        onesf = SB(ctx, "onesf_s", [128, 1])
        convw = SB(ctx, "convw_s", [128, HC * 3], BF16)
        convb = SB(ctx, "convb_s", [128, HC])
        bdt = SB(ctx, "bdt_s", [128, HC])
        dvec = SB(ctx, "dvec_s", [128, HC])
        a_bc = SB(ctx, "a_bc", [128, NS])
        cvec = SB(ctx, "cvec_s", [128, DC])
        siluc = SB(ctx, "siluc", [128, DC])
        modc = SB(ctx, "modc", [128, 6 * DC])
        sxr = SB(ctx, "sxr", [128, 8])
        small = SB(ctx, "small", [1, 64])
        cst1 = SB(ctx, "cst1", [1, 4])
        state = SB(ctx, "state", [128, NS * HC])
        rmaxb = SB(ctx, "rmaxb", [128, 1])
        rpx = SB(ctx, "rpx", [128, 1])
        csc = SB(ctx, "csc", [128, DC])
        wxpT = SB(ctx, "wxpT_s", [128, HC * 96], BF16)
        wdtT = SB(ctx, "wdtT_s", [64, H], BF16)
        wmaxg = SB(ctx, "wmaxg", [128, ICN])
        wmaxu = SB(ctx, "wmaxu", [128, ICN])
        wmaxd = SB(ctx, "wmaxd", [128, DC])
        wsceg = SB(ctx, "wsceg", [128, ICN])
        wsceu = SB(ctx, "wsceu", [128, ICN])

        SL_ALOG, SL_A, SL_CX, SL_CD, SL_GM, SL_SX, SL_RX, SL_T0 = 0, 16, 33, 34, 35, 36, 37, 38
        SL_GM2, SL_SD, SL_RD, SL_T1, SL_SX127, SL_SD127 = 40, 41, 42, 43, 44, 45
        SL_CC = 48

        p0 = ctx.enter_context(ExitStack())
        wmod_buf = SB(p0, "wmod_buf", [128, 1024])
        modsb = SB(p0, "modsb", [1, 6 * D])
        wrow = [SB(p0, f"wrow{i}", [128, NI]) for i in range(2)]

        for _i, _v in ((0, 1e-5), (1, -0.5), (2, -1.0)):
            t_cst = S.add("DVE", lambda e, _a=_i, _b=_v: e.memset(cst1[0:1, _a:_a + 1], _b))
        t_onesf = S.add("DVE", lambda e: e.memset(onesf[:, :], 1.0))

        tload = {}
        for nm, dst, src in (("onesb", onesb, onesb_h), ("convw", convw, convw_h),
                             ("convb", convb, convb_h), ("bdt", bdt, bdt_h),
                             ("dvec", dvec, dvec_h), ("cvec", cvec, cvec_h)):
            _, tload[nm] = S.dma("SP", dst[:, :], src)
        _, tload["wxpT"] = S.dma("POOL", wxpT[:, :],
                                 hap(wxpT_h, 0, [[96, 128], [128 * 96, HC], [1, 96]]))
        _, tload["wdtT"] = S.dma("POOL", wdtT[0:64, :], wdtT_h)
        _, t_alog = S.dma("SP", small[0:1, SL_ALOG:SL_ALOG + NS], alog_h)
        _, t_cx = S.dma("SP", small[0:1, SL_CX:SL_CX + 1], clipx_h)
        _, t_cd = S.dma("SP", small[0:1, SL_CD:SL_CD + 1], clipd_h)

        tA = S.add("ACT", lambda e: e.activation(small[0:1, SL_A:SL_A + NS],
                                                 small[0:1, SL_ALOG:SL_ALOG + NS], AF.Exp), [t_alog])
        tA = S.add("DVE", lambda e: e.tensor_scalar(small[0:1, SL_A:SL_A + NS],
                                                    small[0:1, SL_A:SL_A + NS], -1.0, None, ALU.mult), [tA])
        _, dA1 = S.dma("ACT", sc_h[0:1, 0:NS], small[0:1, SL_A:SL_A + NS], [tA])
        _, t_abc = S.dma("SP", a_bc[:, :], rep128(sc_h[0:1, 0:NS]), [dA1])

        t_sc = S.add("ACT", lambda e: e.activation(siluc[:, :], cvec[:, :], AF.Silu), [tload["cvec"]])
        prev_es_ev = None
        for es in range(12):
            mmt = None
            for kc in range(DC):
                b = (kc % 2) * 512
                _, dtok = S.dma("SP", wmod_buf[:, b:b + 512],
                                wmod_h[kc * 128:(kc + 1) * 128, es * 512:(es + 1) * 512],
                                [mmt] if kc >= 2 else [prev_es_ev])
                mmt = S.add("PE", lambda e, _kc=kc, _b=b: e.matmul(
                    pb[6][0:1, :], siluc[:, _kc:_kc + 1], wmod_buf[:, _b:_b + 512],
                    start=(_kc == 0), stop=(_kc == DC - 1)),
                    [dtok, t_sc, prev_es_ev])
            prev_es_ev = S.add("ACT", lambda e, _es=es: e.copy(
                modsb[0:1, _es * 512:(_es + 1) * 512], pb[6][0:1, :]), [mmt])
        _, t_modw = S.dma("ACT", mod_h[0:1, :], modsb[0:1, :], [prev_es_ev])
        for k in range(6):
            for dc in range(DC):
                S.dma("SP", modc[:, k * DC + dc:k * DC + dc + 1],
                      hap(mod_h, k * 1024 + dc * 128, [[1, 128], [1, 1]]),
                      [t_modw])
        t_modr = S.ring_token("SP")
        t_ca = S.add("DVE", lambda e: e.tensor_scalar(modc[:, DC:2 * DC], modc[:, DC:2 * DC], 1.0, None, ALU.add), [t_modr])
        t_cb = S.add("DVE", lambda e: e.tensor_scalar(modc[:, 4 * DC:5 * DC], modc[:, 4 * DC:5 * DC], 1.0, None, ALU.add), [t_modr])
        gfix_toks = []
        for col in (2 * DC, 5 * DC):
            tg1 = S.add("ACT", lambda e, _c=col: e.activation(sxr[:, 0:DC], modc[:, _c:_c + DC], AF.Abs),
                        [t_modr, gfix_toks[-1] if gfix_toks else None])
            tg2 = S.add("DVE", lambda e: e.tensor_scalar(sxr[:, 0:DC], sxr[:, 0:DC], 0.25, None, ALU.max), [tg1])
            tg3 = S.add("DVE", lambda e, _c=col: e.tensor_scalar(csc[:, 0:DC], modc[:, _c:_c + DC], 0.0, 2.0, ALU.is_ge, ALU.mult), [t_modr])
            tg4 = S.add("DVE", lambda e: e.tensor_scalar(csc[:, 0:DC], csc[:, 0:DC], 1.0, None, ALU.subtract), [tg3])
            tg5 = S.add("DVE", lambda e, _c=col: e.tensor_tensor(modc[:, _c:_c + DC], sxr[:, 0:DC], csc[:, 0:DC], ALU.mult), [tg2, tg4])
            gfix_toks.append(tg5)

        rm_last = {0: None, 1: None}
        rmax_done = []
        for wsb, wh in ((wmaxg, wg_h), (wmaxu, wu_h)):
            for ic in range(ICN):
                par = ic % 2
                _, dw = S.dma("POOL", wrow[par][:, 0:D], wh[ic * 128:(ic + 1) * 128, :],
                              [rm_last[par]])
                rm_last[par] = S.add("DVE", lambda e, _w=wsb, _i=ic, _p=par: e.reduce_max(
                    _w[:, _i:_i + 1], wrow[_p][:, 0:D], axis=AX.X, apply_absolute_value=True),
                    [dw])
            rmax_done.append(rm_last[0])
            rmax_done.append(rm_last[1])
        for dc in range(DC):
            par = dc % 2
            _, dw = S.dma("POOL", wrow[par][:, 0:NI], wd_h[dc * 128:(dc + 1) * 128, :],
                          [rm_last[par]])
            rm_last[par] = S.add("DVE", lambda e, _d=dc, _p=par: e.reduce_max(
                wmaxd[:, _d:_d + 1], wrow[_p][:, 0:NI], axis=AX.X, apply_absolute_value=True),
                [dw])
        rmax_done = [rm_last[0], rm_last[1]]
        trg = S.add("DVE", lambda e: e.reciprocal(wsceg[:, :], wmaxg[:, :]), [rmax_done])
        trg = S.add("DVE", lambda e: e.tensor_scalar(
            wsceg[:, :], wsceg[:, :], 127.0, None, ALU.mult), [trg])
        tru = S.add("DVE", lambda e: e.reciprocal(wsceu[:, :], wmaxu[:, :]), [rmax_done])
        tru = S.add("DVE", lambda e: e.tensor_scalar(
            wsceu[:, :], wsceu[:, :], 127.0, None, ALU.mult), [tru])
        trd = S.add("DVE", lambda e: e.reciprocal(csc[:, 0:DC], wmaxd[:, :]), [rmax_done, gfix_toks[-1]])
        trd = S.add("DVE", lambda e: e.tensor_scalar(
            csc[:, 0:DC], csc[:, 0:DC], 127.0, None, ALU.mult), [trd])
        drg = dru = drd = None
        for ic in range(ICN):
            _, drg = S.dma("ACT", hap(rsg_h, ic * 128, [[1, 128]]), wsceg[:, ic:ic + 1], [trg])
            _, dru = S.dma("ACT", hap(rsu_h, ic * 128, [[1, 128]]), wsceu[:, ic:ic + 1], [tru])
        for dc in range(DC):
            _, drd = S.dma("ACT", hap(rsd_h, dc * 128, [[1, 128]]), csc[:, dc:dc + 1], [trd])
        rs_bounced = [drg, dru, drd]

        if stop_after == "P0":
            S.add("SP", lambda e: e.nop(nofuse=True), [S.ring_token("ACT"), S.ring_token("SP"), S.ring_token("POOL")])
            S.emit(nc, ctx)
            return nc
        S.barrier()
        p0.close()

        pm = ctx.enter_context(ExitStack())
        xbf = SB(pm, "xbf", [128, DC * T], BF16)
        xsq = SB(pm, "xsq", [128, T], BF16)
        xsqf = SB(pm, "xsqf", [128, T])
        h1c = [SB(pm, f"h1c{i}", [128, DC * T], BF16) for i in range(2)]
        mrbc = SB(pm, "mrbc", [128, 2 * T], BF16)
        stats = SB(pm, "stats", [1, 3 * T])
        stb16 = SB(pm, "stb16", [1, 2 * T], BF16)
        u_ext = [SB(pm, f"u_ext{i}", [128, HC * UE], BF16) for i in range(2)]
        zsp = [SB(pm, f"zsp{i}", [128, T], BF16) for i in range(2)]
        zld = [SB(pm, f"zld{i}", [128, GH * T], BF16) for i in range(1)]
        ucb = SB(pm, "ucb", [128, HC * T], BF16)
        dtb = SB(pm, "dtb", [128, HC * T], BF16)
        winb = [SB(pm, f"winb{i}", [128, DC * 128], BF16) for i in range(2)]
        wob = [SB(pm, f"wob{i}", [128, HC * 128], BF16) for i in range(2)]
        dtpb = [SB(pm, f"dtpb{i}", [64, T], BF16) for i in range(2)]
        bcb = [SB(pm, f"bcb{i}", [32, T], BF16) for i in range(2)]
        dAb = [SB(pm, f"dAb{i}", [128, GH * T]) for i in range(2)]
        bub = [SB(pm, f"bub{i}", [128, GH * T], BF16) for i in range(2)]
        hsb = [SB(pm, f"hsb{i}", [128, GH * T], BF16) for i in range(2)]
        tmpb = SB(pm, "tmpb", [128, GH * T], BF16)
        ebuf = [SB(pm, f"ebuf{i}", [128, T], BF16) for i in range(2)]
        bbc = [SB(pm, f"bbc{i}", [128, GH * T], BF16) for i in range(2)]
        cbc = [SB(pm, f"cbc{i}", [128, GH * T], BF16) for i in range(2)]
        y_acc = SB(pm, "y_acc", [128, HC * T], BF16)
        x2b = SB(pm, "x2b", [128, DC * T])
        xres = [SB(pm, f"xres{i}", [128, T]) for i in range(2)]
        e1f = [SB(pm, f"e1f{i}", [128, T]) for i in range(2)]
        h2sl = [SB(pm, f"h2sl{i}", [128, T]) for i in range(2)]
        m2bc = SB(pm, "m2bc", [128, 2 * T])

        tzs = S.add("DVE", lambda e: e.memset(state[:, :], 0.0))
        tzr = S.add("DVE", lambda e: e.memset(rmaxb[:, :], 0.0))

        rot = {"i": 0}
        bank_guard = {}

        def next_bank():
            b = rot["i"] % 4
            rot["i"] += 1
            return b, bank_guard.get(b)

        cross = {}
        conv_deps_of = {}

        def stage_A(c):
            t0 = c * T
            cur = c % 2
            _, dx = S.dma("POOL", xbf[:, :],
                          hap(x_h, t0, [[L, 128], [128 * L, DC], [1, T]]),
                          [cross.get("xbf_free")])
            msum = None
            msq = None
            for dc in range(DC):
                tq = S.add("ACT", lambda e, _dc=dc: e.activation(
                    xsq[:, :], xbf[:, _dc * T:(_dc + 1) * T], AF.Square),
                    [dx, cross.get("xsq_free"), msq])
                msum = S.add("PE", lambda e, _dc=dc: e.matmul(
                    pb[4][0:1, :], onesb[:, :], xbf[:, _dc * T:(_dc + 1) * T],
                    start=(_dc == 0), stop=(_dc == DC - 1)),
                    [dx, tload["onesb"], cross.get("b4_free") if dc == 0 else None, msum])
                msq = S.add("PE", lambda e, _dc=dc: e.matmul(
                    pb[5][0:1, :], onesb[:, :], xsq[:, :],
                    start=(_dc == 0), stop=(_dc == DC - 1)),
                    [tq, cross.get("b5_free") if dc == 0 else None, msq])
            cross["xsq_free"] = msq
            tm = S.add("DVE", lambda e: e.tensor_scalar(
                stats[0:1, 0:T], pb[4][0:1, :], 1.0 / D, None, ALU.mult),
                [msum, cross.get("stats_free")])
            cross["b4_free"] = tm
            tmsq = S.add("ACT", lambda e: e.activation(
                stats[0:1, 2 * T:3 * T], stats[0:1, 0:T], AF.Square), [tm])
            tv = S.add("DVE", lambda e: e.scalar_tensor_tensor(
                stats[0:1, T:2 * T], pb[5][0:1, :], 1.0 / D,
                stats[0:1, 2 * T:3 * T], ALU.mult, ALU.subtract), [msq, tmsq])
            cross["b5_free"] = tv
            tl = S.add("ACT", lambda e: e.activation(
                stats[0:1, 2 * T:3 * T], stats[0:1, T:2 * T], AF.Ln,
                bias=cst1[0:1, 0:1]), [tv, t_cst])
            tr = S.add("ACT", lambda e: e.activation(
                stats[0:1, 2 * T:3 * T], stats[0:1, 2 * T:3 * T], AF.Exp,
                scale=cst1[0:1, 1:2]), [tl, t_cst])
            tb1 = S.add("DVE", lambda e: e.tensor_scalar(
                stb16[0:1, 0:T], stats[0:1, 0:T], 1.0, None, ALU.mult), [tm, cross.get("stb_free")])
            tb2 = S.add("DVE", lambda e: e.tensor_scalar(
                stb16[0:1, T:2 * T], stats[0:1, 2 * T:3 * T], 1.0, None, ALU.mult), [tr, cross.get("stb_free")])
            cross["stats_free"] = [tb1, tb2]
            _, dm1 = S.dma("ACT", st1_h[0:1, t0:t0 + T], stb16[0:1, 0:T], [tb1])
            _, dm2 = S.dma("ACT", st1_h[0:1, L + t0:L + t0 + T], stb16[0:1, T:2 * T], [tb2])
            cross["stb_free"] = [dm1, dm2]
            _, dmb1 = S.dma("SP", mrbc[:, 0:T], rep128(hap(st1_h, t0, [[1, T]])),
                            [dm1, cross.get("mrbc_free")])
            _, dmb2 = S.dma("SP", mrbc[:, T:2 * T], rep128(hap(st1_h, L + t0, [[1, T]])),
                            [dm2, cross.get("mrbc_free")])
            tln1 = S.add("DVE", lambda e, _cu=cur: e.tensor_tensor(
                ap3(h1c[_cu][:, :], 0, [[T, DC], [1, T]]),
                ap3(xbf[:, :], 0, [[T, DC], [1, T]]),
                ap3(mrbc[:, :], 0, [[0, DC], [1, T]]), ALU.subtract),
                [dx, dmb1, cross.get(f"h1_free{cur}")])
            cross["xbf_free"] = tln1
            tln2 = S.add("DVE", lambda e, _cu=cur: e.tensor_tensor(
                ap3(h1c[_cu][:, :], 0, [[T, DC], [1, T]]),
                ap3(h1c[_cu][:, :], 0, [[T, DC], [1, T]]),
                ap3(mrbc[:, :], T, [[0, DC], [1, T]]), ALU.mult), [tln1, dmb2])
            cross["mrbc_free"] = tln2
            th1 = []
            for dc in range(DC):
                th1.append(S.add("DVE", lambda e, _dc=dc, _cu=cur: e.tensor_scalar(
                    h1c[_cu][:, _dc * T:(_dc + 1) * T],
                    h1c[_cu][:, _dc * T:(_dc + 1) * T],
                    modc[:, DC + _dc:DC + _dc + 1], modc[:, _dc:_dc + 1],
                    ALU.mult, ALU.add),
                    [tln2, t_ca, t_modr]))
            chunk_state[c] = {"th1": th1, "cur": cur}

        def stage_xz(c, ec0, ec1):
            cur = c % 2
            st = chunk_state[c]
            th1 = st["th1"]
            ev_u = st.setdefault("ev_u", [])
            ev_z = st.setdefault("ev_z", [])
            zdmas = st.setdefault("zdmas", [])
            for ec in range(ec0, ec1):
                wpar = ec % 2
                _, dwt = S.dma("POOL", winb[wpar][:, :],
                               hap(w_inT_h, ec * 128, [[E, 128], [128 * E, DC], [1, 128]]),
                               [cross.get(f"winb_free{wpar}")])
                bk, bg = next_bank()
                mmt = None
                for kc in range(DC):
                    mmt = S.add("PE", lambda e, _ec=ec, _kc=kc, _w=wpar, _b=bk, _cu=cur: e.matmul(
                        pb[_b][:, :],
                        winb[_w][:, _kc * 128:(_kc + 1) * 128],
                        h1c[_cu][:, _kc * T:(_kc + 1) * T],
                        start=(_kc == 0), stop=(_kc == DC - 1)),
                        [th1[kc], dwt, bg if kc == 0 else None, mmt])
                if ec < HC:
                    tev = S.add("ACT", lambda e, _ec=ec, _b=bk, _cu=cur: e.copy(
                        u_ext[_cu][:, _ec * UE + 1:_ec * UE + 1 + T], pb[_b][:, :]),
                        [mmt, cross.get(f"uext_free{cur}") if ec == 0 else None])
                    ev_u.append(tev)
                else:
                    zpar = ec % 2
                    tev = S.add("ACT", lambda e, _b=bk, _z=zpar: e.copy(
                        zsp[_z][:, :], pb[_b][:, :]),
                        [mmt, cross.get(f"zsp_free{zpar}")])
                    _, dz = S.dma("ACT", z_h[(ec - HC) * 128:(ec - HC + 1) * 128,
                                             c * T:c * T + T],
                                  zsp[zpar][:, :], [tev])
                    cross[f"zsp_free{zpar}"] = dz
                    zdmas.append(dz)
                bank_guard[bk] = tev
                cross[f"winb_free{wpar}"] = mmt
            if ec1 == ECN:
                cross[f"z_spilled{cur}"] = zdmas
                cross["uext_prev_done_full"] = ev_u[-1]

        def stage_edges(c):
            cur = c % 2
            ev_u = chunk_state[c]["ev_u"]
            if c == 0:
                tle = S.add("DVE", lambda e, _cu=cur: e.memset(
                    ap3(u_ext[_cu][:, :], 0, [[UE, HC], [1, 1]]), 0.0),
                    [cross.get(f"uext_free{cur}")])
            else:
                tle = S.add("DVE", lambda e, _cu=cur: e.tensor_scalar(
                    ap3(u_ext[_cu][:, :], 0, [[UE, HC], [1, 1]]),
                    ap3(u_ext[1 - _cu][:, :], T, [[UE, HC], [1, 1]]),
                    1.0, None, ALU.mult),
                    [cross["uext_prev_done"], cross.get(f"uext_free{cur}")])
            if c > 0:
                tre = S.add("DVE", lambda e, _cu=cur: e.tensor_scalar(
                    ap3(u_ext[1 - _cu][:, :], T + 1, [[UE, HC], [1, 1]]),
                    ap3(u_ext[_cu][:, :], 1, [[UE, HC], [1, 1]]),
                    1.0, None, ALU.mult), [ev_u[0], cross["uext_prev_done"]])
                conv_deps_of[c - 1] = [cross["uext_prev_done"], tre, cross[f"le_tok{1 - cur}"]]
            cross[f"le_tok{cur}"] = tle

        bstate = {}

        def stage_B_front(cc):
            t0 = cc * T
            g = cc % 2
            deps = conv_deps_of.pop(cc)
            tcv = S.add("DVE", lambda e, _g=g: e.tensor_tensor(
                ap3(ucb[:, :], 0, [[T, HC], [1, T]]),
                ap3(u_ext[_g][:, :], 0, [[UE, HC], [1, T]]),
                ap3(convw[:, :], 0, [[3, HC], [0, T]]), ALU.mult),
                deps + [tload["convw"], cross.get("uc_free")])
            tc_toks = []
            tprev = tcv
            for q in range(NG):
                c0 = q * GH * T
                tcv2 = S.add("DVE", lambda e, _g=g, _q=q: e.tensor_tensor(
                    ap3(tmpb[:, :], 0, [[T, GH], [1, T]]),
                    ap3(u_ext[_g][:, :], _q * GH * UE + 1, [[UE, GH], [1, T]]),
                    ap3(convw[:, :], _q * GH * 3 + 1, [[3, GH], [0, T]]), ALU.mult),
                    [tprev, cross.get("tmpb_free") if q == 0 else None])
                ta1 = S.add("DVE", lambda e, _c0=c0: e.tensor_tensor(
                    ucb[:, _c0:_c0 + GH * T], ucb[:, _c0:_c0 + GH * T],
                    tmpb[:, 0:GH * T], ALU.add), [tcv2])
                tcv3 = S.add("DVE", lambda e, _g=g, _q=q: e.tensor_tensor(
                    ap3(tmpb[:, :], 0, [[T, GH], [1, T]]),
                    ap3(u_ext[_g][:, :], _q * GH * UE + 2, [[UE, GH], [1, T]]),
                    ap3(convw[:, :], _q * GH * 3 + 2, [[3, GH], [0, T]]), ALU.mult), [ta1])
                ta2 = S.add("DVE", lambda e, _c0=c0: e.tensor_tensor(
                    ucb[:, _c0:_c0 + GH * T], ucb[:, _c0:_c0 + GH * T],
                    tmpb[:, 0:GH * T], ALU.add), [tcv3])
                tc_toks.append(ta2)
                tprev = ta2
            cross[f"uext_free{g}"] = tc_toks[-1]
            cross["tmpb_free"] = tc_toks[-1]
            tsil = []
            for hc in range(HC):
                tsil.append(S.add("ACT", lambda e, _hc=hc: e.activation(
                    ucb[:, _hc * T:(_hc + 1) * T], ucb[:, _hc * T:(_hc + 1) * T],
                    AF.Silu, bias=convb[:, _hc:_hc + 1]),
                    [tc_toks[hc // GH], tload["convb"]]))
            bk, bg = next_bank()
            mmx = None
            for hc in range(HC):
                mmx = S.add("PE", lambda e, _hc=hc, _b=bk: e.matmul(
                    pb[_b][0:96, :], wxpT[:, _hc * 96:(_hc + 1) * 96],
                    ucb[:, _hc * T:(_hc + 1) * T],
                    start=(_hc == 0), stop=(_hc == HC - 1)),
                    [tsil[hc], tload["wxpT"], bg if hc == 0 else None, mmx])
            par = cc % 2
            tdtp = S.add("ACT", lambda e, _p=par, _b=bk: e.copy(dtpb[_p][0:64, :], pb[_b][0:64, :]),
                         [mmx, cross.get(f"dtp_free{par}")])
            tbc = S.add("ACT", lambda e, _p=par, _b=bk: e.copy(bcb[_p][0:32, :], pb[_b][64:96, :]),
                        [mmx, cross.get(f"bcb_free{par}")])
            bank_guard[bk] = tbc
            _, dbc = S.dma("ACT", bc_h[0:32, t0:t0 + T], bcb[par][0:32, :], [tbc])
            cross[f"bcb_free{par}"] = dbc
            tdt = []
            for hc in range(HC):
                bk2, bg2 = next_bank()
                mmd = S.add("PE", lambda e, _hc=hc, _b=bk2, _p=par: e.matmul(
                    pb[_b][:, :], wdtT[0:64, _hc * 128:(_hc + 1) * 128], dtpb[_p][0:64, :],
                    start=True, stop=True),
                    [tdtp, tload["wdtT"], bg2])
                ep2 = hc % 2
                te0 = S.add("ACT", lambda e, _hc=hc, _b=bk2, _e2=ep2: e.activation(
                    ebuf[_e2][:, :], pb[_b][:, :], AF.Exp,
                    bias=bdt[:, _hc:_hc + 1]),
                    [mmd, tload["bdt"], cross.get(f"ebuf_free{ep2}")])
                bank_guard[bk2] = te0
                te1 = S.add("ACT", lambda e, _hc=hc, _e2=ep2: e.activation(
                    dtb[:, _hc * T:(_hc + 1) * T], ebuf[_e2][:, :], AF.Ln,
                    bias=1.0), [te0])
                cross[f"ebuf_free{ep2}"] = te1
                tdt.append(te1)
            cross[f"dtp_free{par}"] = tdt[-1]
            bstate[cc] = {"tsil": tsil, "tdt": tdt, "dbc": dbc}

        def stage_B_scan(cc, feeder=None):
            t0 = cc * T
            g = cc % 2
            par = cc % 2
            bs = bstate.pop(cc)
            tsil, tdt, dbc = bs["tsil"], bs["tdt"], bs["dbc"]
            ydeps = [cross.get("y_used")]
            for n in range(NS):
                if feeder is not None:
                    feeder(n)
                npar = n % 2
                _, dbb = S.dma("SP", bbc[npar][:, :],
                               hap(bc_h, n * L + t0, [[0, 128], [0, GH], [1, T]]),
                               [cross.get(f"bbc_free{npar}"), dbc])
                _, dcb = S.dma("SP", cbc[npar][:, :],
                               hap(bc_h, (NS + n) * L + t0, [[0, 128], [0, GH], [1, T]]),
                               [cross.get(f"cbc_free{npar}"), dbc])
                for q in range(NG):
                    it = n * NG + q
                    spar = it % 2
                    col0 = n * HC + q * GH
                    tda = S.add("ACT", lambda e, _q=q, _p=spar, _n=n: e.activation(
                        dAb[_p][:, :], dtb[:, _q * GH * T:(_q + 1) * GH * T],
                        AF.Exp, scale=a_bc[:, _n:_n + 1]),
                        [tdt[q * GH + GH - 1] if n == 0 else None, t_abc,
                         cross.get(f"dA_free{spar}")])
                    tbu = S.add("DVE", lambda e, _q=q, _p=spar, _np=npar: e.tensor_tensor(
                        bub[_p][:, :], ucb[:, _q * GH * T:(_q + 1) * GH * T],
                        bbc[_np][:, :], ALU.mult),
                        [tsil[(q + 1) * GH - 1] if n == 0 else None, dbb,
                         cross.get(f"bu_free{spar}")])
                    tscan = []
                    for j in range(GH):
                        tscan.append(S.add("DVE", lambda e, _j=j, _p=spar, _c=col0: e.tensor_tensor_scan(
                            hsb[_p][:, _j * T:(_j + 1) * T],
                            dAb[_p][:, _j * T:(_j + 1) * T],
                            bub[_p][:, _j * T:(_j + 1) * T],
                            state[:, _c + _j:_c + _j + 1], ALU.mult, ALU.add),
                            [tda, tbu, tzs,
                             cross.get(f"carry{col0 + j}"),
                             cross.get(f"hs_free{spar}") if j == 0 else None]))
                    tcar = S.add("ACT", lambda e, _p=spar, _c=col0: e.copy(
                        state[:, _c:_c + GH],
                        ap3(hsb[_p][:, :], T - 1, [[T, GH], [1, 1]])), [tscan[-1]])
                    for j in range(GH):
                        cross[f"carry{col0 + j}"] = tcar
                    cross[f"dA_free{spar}"] = tscan[-1]
                    cross[f"bu_free{spar}"] = tscan[-1]
                    if n == 0:
                        tct = S.add("DVE", lambda e, _q=q, _p=spar, _np=npar: e.tensor_tensor(
                            y_acc[:, _q * GH * T:(_q + 1) * GH * T],
                            hsb[_p][:, :], cbc[_np][:, :], ALU.mult),
                            [tscan[-1], dcb] + (ydeps if q == 0 else []))
                        cross[f"hs_free{spar}"] = tct
                        tya = tct
                    else:
                        tct = S.add("DVE", lambda e, _p=spar, _np=npar: e.tensor_tensor(
                            tmpb[:, :], hsb[_p][:, :], cbc[_np][:, :], ALU.mult),
                            [tscan[-1], dcb, cross.get("tmpb_free")])
                        cross[f"hs_free{spar}"] = tct
                        tya = S.add("DVE", lambda e, _q=q: e.tensor_tensor(
                            y_acc[:, _q * GH * T:(_q + 1) * GH * T],
                            y_acc[:, _q * GH * T:(_q + 1) * GH * T],
                            tmpb[:, 0:GH * T], ALU.add), [tct, cross.get(f"ya{q}")])
                        cross["tmpb_free"] = tya
                    cross[f"ya{q}"] = tya
                cross[f"bbc_free{npar}"] = cross[f"bu_free{(n * NG + NG - 1) % 2}"]
                cross[f"cbc_free{npar}"] = cross[f"ya{NG - 1}"]
            ylast = [cross[f"ya{q}"] for q in range(NG)]
            yfin = []
            for q in range(NG):
                zpar = 0
                _, dzl = S.dma("SP", zld[zpar][:, :],
                               hap(z_h, q * GH * 128 * L + t0, [[L, 128], [128 * L, GH], [1, T]]),
                               [cross.get(f"zld_free{zpar}")] + cross[f"z_spilled{g}"])
                tzsil = S.add("ACT", lambda e, _z=zpar: e.activation(
                    zld[_z][:, :], zld[_z][:, :], AF.Silu), [dzl])
                tduc = []
                for j in range(GH):
                    hc = q * GH + j
                    tduc.append(S.add("DVE", lambda e, _hc=hc, _j=j: e.tensor_scalar(
                        tmpb[:, _j * T:(_j + 1) * T], ucb[:, _hc * T:(_hc + 1) * T],
                        dvec[:, _hc:_hc + 1], None, ALU.mult),
                        [tload["dvec"], cross.get("tmpb_free") if j == 0 else None]))
                ta = S.add("DVE", lambda e, _q=q: e.tensor_tensor(
                    y_acc[:, _q * GH * T:(_q + 1) * GH * T],
                    y_acc[:, _q * GH * T:(_q + 1) * GH * T],
                    tmpb[:, 0:GH * T], ALU.add), [tduc[-1], ylast[q]])
                cross["tmpb_free"] = ta
                tm2 = S.add("DVE", lambda e, _q=q, _z=zpar: e.tensor_tensor(
                    y_acc[:, _q * GH * T:(_q + 1) * GH * T],
                    y_acc[:, _q * GH * T:(_q + 1) * GH * T],
                    zld[_z][:, 0:GH * T], ALU.mult), [ta, tzsil])
                cross[f"zld_free{zpar}"] = tm2
                yfin.append(tm2)
            cross["uc_free"] = yfin[-1]
            tx2 = []
            for dc in range(DC):
                wpar = dc % 2
                _, dwo = S.dma("POOL", wob[wpar][:, :],
                               hap(woutT_h, dc * 128, [[D, 128], [128 * D, HC], [1, 128]]),
                               [cross.get(f"wob_free{wpar}")])
                epar = dc % 2
                _, dxr = S.dma("SP", xres[epar][:, :],
                               x_h[dc * 128:(dc + 1) * 128, t0:t0 + T],
                               [cross.get(f"xres_free{epar}")])
                bk3, bg3 = next_bank()
                mmt = None
                for kc in range(HC):
                    mmt = S.add("PE", lambda e, _dc=dc, _kc=kc, _w=wpar, _b=bk3: e.matmul(
                        pb[_b][:, :], wob[_w][:, _kc * 128:(_kc + 1) * 128],
                        y_acc[:, _kc * T:(_kc + 1) * T],
                        start=(_kc == 0), stop=(_kc == HC - 1)),
                        [yfin[kc // GH], dwo, bg3 if kc == 0 else None, mmt])
                cross[f"wob_free{wpar}"] = mmt
                te1 = S.add("DVE", lambda e, _dc=dc, _b=bk3, _e=epar, _g=g: e.tensor_tensor(
                    e1f[_e][:, :], pb[_b][:, :], h1c[_g][:, _dc * T:(_dc + 1) * T], ALU.add),
                    [mmt, cross.get(f"e1_free{epar}")])
                bank_guard[bk3] = te1
                tx = S.add("DVE", lambda e, _dc=dc, _e=epar: e.scalar_tensor_tensor(
                    x2b[:, _dc * T:(_dc + 1) * T], e1f[_e][:, :],
                    modc[:, 2 * DC + _dc:2 * DC + _dc + 1],
                    xres[_e][:, :], ALU.mult, ALU.add),
                    [te1, dxr, gfix_toks[0], cross.get("x2b_free")])
                cross[f"e1_free{epar}"] = tx
                cross[f"xres_free{epar}"] = tx
                tx2.append(tx)
            cross[f"h1_free{g}"] = tx2[-1]
            cross["y_used"] = tx2[-1]
            _, dx2 = S.dma("ACT", hap(x2_h, t0, [[L, 128], [128 * L, DC], [1, T]]),
                           x2b[:, :], [tx2[-1]])
            msum = None
            msq = None
            for dc in range(DC):
                tq = S.add("ACT", lambda e, _dc=dc: e.activation(
                    xsqf[:, :], x2b[:, _dc * T:(_dc + 1) * T], AF.Square),
                    [tx2[dc], msq, cross.get("xsqf_free")])
                msum = S.add("PE", lambda e, _dc=dc: e.matmul(
                    pb[6][0:1, :], onesf[:, :], x2b[:, _dc * T:(_dc + 1) * T],
                    start=(_dc == 0), stop=(_dc == DC - 1)),
                    [tx2[dc], t_onesf, cross.get("b6_free") if dc == 0 else None, msum])
                msq = S.add("PE", lambda e, _dc=dc: e.matmul(
                    pb[7][0:1, :], onesf[:, :], xsqf[:, :],
                    start=(_dc == 0), stop=(_dc == DC - 1)),
                    [tq, cross.get("b7_free") if dc == 0 else None, msq])
            cross["xsqf_free"] = msq
            tm = S.add("DVE", lambda e: e.tensor_scalar(
                stats[0:1, 0:T], pb[6][0:1, :], 1.0 / D, None, ALU.mult),
                [msum, cross.get("stats_free")])
            cross["b6_free"] = tm
            tmsq = S.add("ACT", lambda e: e.activation(
                stats[0:1, 2 * T:3 * T], stats[0:1, 0:T], AF.Square), [tm])
            tv = S.add("DVE", lambda e: e.scalar_tensor_tensor(
                stats[0:1, T:2 * T], pb[7][0:1, :], 1.0 / D,
                stats[0:1, 2 * T:3 * T], ALU.mult, ALU.subtract), [msq, tmsq])
            cross["b7_free"] = tv
            tl = S.add("ACT", lambda e: e.activation(
                stats[0:1, 2 * T:3 * T], stats[0:1, T:2 * T], AF.Ln,
                bias=cst1[0:1, 0:1]), [tv, t_cst])
            tr = S.add("ACT", lambda e: e.activation(
                stats[0:1, 2 * T:3 * T], stats[0:1, 2 * T:3 * T], AF.Exp,
                scale=cst1[0:1, 1:2]), [tl, t_cst])
            _, dm1 = S.dma("ACT", st2_h[0:1, t0:t0 + T], stats[0:1, 0:T], [tm])
            _, dm2 = S.dma("ACT", st2_h[0:1, L + t0:L + t0 + T], stats[0:1, 2 * T:3 * T], [tr])
            cross["stats_free"] = [dm1, dm2]
            _, db1 = S.dma("SP", m2bc[:, 0:T], rep128(hap(st2_h, t0, [[1, T]])),
                           [dm1, cross.get("m2bc_free")])
            _, db2 = S.dma("SP", m2bc[:, T:2 * T], rep128(hap(st2_h, L + t0, [[1, T]])),
                           [dm2, cross.get("m2bc_free")])
            m2toks = []
            for dc in range(DC):
                epar = dc % 2
                ts1 = S.add("DVE", lambda e, _dc=dc, _e=epar: e.tensor_tensor(
                    h2sl[_e][:, :], x2b[:, _dc * T:(_dc + 1) * T],
                    m2bc[:, 0:T], ALU.subtract),
                    [db1, cross.get(f"h2sl_free{epar}")])
                ts2 = S.add("DVE", lambda e, _e=epar: e.tensor_tensor(
                    h2sl[_e][:, :], h2sl[_e][:, :], m2bc[:, T:2 * T], ALU.mult), [ts1, db2])
                ts3 = S.add("DVE", lambda e, _dc=dc, _e=epar: e.tensor_scalar(
                    h2sl[_e][:, :], h2sl[_e][:, :],
                    modc[:, 4 * DC + _dc:4 * DC + _dc + 1],
                    modc[:, 3 * DC + _dc:3 * DC + _dc + 1], ALU.mult, ALU.add),
                    [ts2, t_cb])
                _, dh2 = S.dma("ACT", h2_h[dc * 128:(dc + 1) * 128, t0:t0 + T],
                               h2sl[epar][:, :], [ts3])
                trm = S.add("DVE", lambda e, _e=epar: e.reduce_max(
                    e1f[_e][:, 0:1], h2sl[_e][:, :], axis=AX.X, apply_absolute_value=True),
                    [ts3, cross.get(f"e1_free{epar}")])
                trm2 = S.add("DVE", lambda e, _e=epar: e.tensor_tensor(
                    rmaxb[:, :], rmaxb[:, :], e1f[_e][:, 0:1], ALU.max),
                    [trm, tzr, cross.get("rmax_ser")])
                cross["rmax_ser"] = trm2
                cross[f"e1_free{epar}"] = trm2
                cross[f"h2sl_free{epar}"] = [dh2, trm]
                m2toks.append(trm2)
            cross["m2bc_free"] = m2toks[-1]
            cross["x2b_free"] = [m2toks[-1], dx2]
            cross["rmax"] = m2toks[-1]

        chunk_state = {}
        for c in range(NCH + 1):
            if c < NCH:
                stage_A(c)
                stage_xz(c, 0, 1)
                stage_edges(c)
            if c == 0:
                stage_xz(0, 1, ECN)
                cross["uext_prev_done"] = cross.pop("uext_prev_done_full")
                continue
            cc = c - 1
            if cc == NCH - 1:
                lastg = cc % 2
                tre = S.add("DVE", lambda e, _g=lastg: e.memset(
                    ap3(u_ext[_g][:, :], T + 1, [[UE, HC], [1, 1]]), 0.0),
                    [cross["uext_prev_done"]])
                conv_deps_of[cc] = [cross["uext_prev_done"], tre, cross[f"le_tok{lastg}"]]
            stage_B_front(cc)
            if c < NCH:
                def feeder(n, _c=c):
                    e0 = 1 + 2 * n
                    e1 = min(e0 + 2, ECN)
                    if e0 < ECN:
                        stage_xz(_c, e0, e1)
            else:
                feeder = None
            stage_B_scan(cc, feeder)
            if c < NCH:
                cross["uext_prev_done"] = cross.pop("uext_prev_done_full")

        mix_ring = [S.ring_token("ACT"), S.ring_token("SP")]
        if stop_after == "MIX":
            S.add("SP", lambda e: e.nop(nofuse=True), mix_ring + [S.ring_token("POOL")])
            S.emit(nc, ctx)
            return nc
        S.barrier()
        pm.close()

        def allgather_scalar(rmax_sb, cc_in, cc_out, slot_gmax, dep):
            _, d0 = S.dma("ACT", hap(sc_h, 64, [[1, 128]]), rmax_sb[:, :], [dep])
            _, d1 = S.dma("SP", stats0[0:1, 0:128], hap(sc_h, 64, [[1, 128]]), [d0])
            t2 = S.add("DVE", lambda e: e.reduce_max(
                small[0:1, slot_gmax:slot_gmax + 1], stats0[0:1, 0:128], axis=AX.X), [d1])
            _, d2 = S.dma("ACT", cc_in, small[0:1, slot_gmax:slot_gmax + 1], [t2])
            t3 = S.add("POOL", lambda e: e.collective_compute(
                "AllGather", ALU.bypass, replica_groups=[list(range(NB))],
                ins=[cc_in], outs=[cc_out]), [d2])
            _, d3 = S.dma("SP", small[0:1, SL_CC:SL_CC + 8], cc_out, [t3])
            t4 = S.add("DVE", lambda e: e.reduce_max(
                small[0:1, slot_gmax:slot_gmax + 1], small[0:1, SL_CC:SL_CC + 8], axis=AX.X), [d3])
            return t4

        pf = ctx.enter_context(ExitStack())
        stats0 = SB(pf, "stats0", [1, 128])
        xq = SB(pf, "xq", [128, DC * L], BF16)
        wqg = SB(pf, "wqg", [128, DC * NI], BF16)
        wqu = SB(pf, "wqu", [128, DC * NI], BF16)
        sgb = [SB(pf, f"sgb{i}", [128, T]) for i in range(2)]
        usb = [SB(pf, f"usb{i}", [128, T]) for i in range(2)]
        pbuf = [SB(pf, f"pbuf{i}", [128, T]) for i in range(2)]
        pfq = pf.enter_context(ExitStack())
        h2r = [SB(pfq, f"h2r{i}", [128, 4 * T]) for i in range(2)]
        wldf = [SB(pfq, f"wldf{i}", [128, NI // 2]) for i in range(2)]
        rsrep = SB(pfq, "rsrep", [128, NI // 2])

        tg1 = allgather_scalar(rmaxb, cc_in1, cc_out1, SL_GM, cross["rmax"])
        ts1 = S.add("ACT", lambda e: e.activation(small[0:1, SL_T0:SL_T0 + 1],
                                                  small[0:1, SL_CX:SL_CX + 1], AF.Sigmoid), [t_cx])
        ts2 = S.add("DVE", lambda e: e.tensor_tensor(
            small[0:1, SL_SX:SL_SX + 1], small[0:1, SL_GM:SL_GM + 1],
            small[0:1, SL_T0:SL_T0 + 1], ALU.mult), [tg1, ts1])
        ts3 = S.add("DVE", lambda e: e.tensor_scalar(
            small[0:1, SL_SX:SL_SX + 1], small[0:1, SL_SX:SL_SX + 1], 1.0 / 127.0, None, ALU.mult), [ts2])
        ts4 = S.add("DVE", lambda e: e.reciprocal(small[0:1, SL_RX:SL_RX + 1],
                                                  small[0:1, SL_SX:SL_SX + 1]), [ts3])
        ts6 = S.add("DVE", lambda e: e.tensor_scalar(
            small[0:1, SL_SX127:SL_SX127 + 1], small[0:1, SL_SX:SL_SX + 1], 1.0 / 127.0, None, ALU.mult), [ts3])
        _, dsx = S.dma("ACT", sc_h[0:1, 16:18],
                       ap3(small[0:1, :], SL_RX, [[SL_SX127 - SL_RX, 2], [1, 1]]), [ts4, ts6],
                       allow_slow_non_contiguous=True)
        _, dsxr = S.dma("SP", sxr[:, 0:2], rep128(sc_h[0:1, 16:18]), [dsx])

        txq = {}
        for ci in range(2 * NCH):
            c, half = ci // 2, ci % 2
            t0 = c * T
            par = ci % 2
            _, dh2r = S.dma("SP", h2r[par][:, :],
                            hap(h2_h, half * 4 * 128 * L + t0, [[L, 128], [128 * L, 4], [1, T]]),
                            [txq.get(par)])
            tq1 = S.add("DVE", lambda e, _p=par: e.tensor_scalar(
                h2r[_p][:, :], h2r[_p][:, :], sxr[:, 0:1], MAGIC, ALU.mult, ALU.add), [dh2r, dsxr])
            tq2 = S.add("DVE", lambda e, _p=par: e.tensor_scalar(
                h2r[_p][:, :], h2r[_p][:, :], MAGIC, -128.0, ALU.subtract, ALU.max), [tq1])
            tq3 = S.add("DVE", lambda e, _t0=t0, _hf=half, _p=par: e.tensor_scalar(
                ap3(xq[:, :], _hf * 4 * L + _t0, [[L, 4], [1, T]]), h2r[_p][:, :],
                127.0, None, ALU.min), [tq2])
            txq[par] = tq3
        txq_all = [txq[0], txq[1]]

        NH = NI // 2
        wq_toks = {"g": [[None, None] for _ in range(DC)],
                   "u": [[None, None] for _ in range(DC)]}
        drs_prev = None
        for nm, wTh, rs_h_, wq_sb, dbounce in (("g", wgT_h, rsg_h, wqg, rs_bounced[0]),
                                               ("u", wuT_h, rsu_h, wqu, rs_bounced[1])):
            for hf in range(2):
                _, drs = S.dma("SP", rsrep[:, :], rep128(hap(rs_h_, hf * NH, [[1, NH]])),
                               [dbounce, drs_prev])
                last_q = None
                for dc in range(DC):
                    par = dc % 2
                    _, dwt = S.dma("POOL", wldf[par][:, :],
                                   wTh[dc * 128:(dc + 1) * 128, hf * NH:(hf + 1) * NH],
                                   [cross.get(f"wldf_free{par}")])
                    tq1 = S.add("DVE", lambda e, _p=par: e.tensor_tensor(
                        wldf[_p][:, :], wldf[_p][:, :], rsrep[:, :],
                        ALU.mult), [dwt, drs])
                    tq2 = S.add("DVE", lambda e, _p=par: e.tensor_scalar(
                        wldf[_p][:, :], wldf[_p][:, :], MAGIC, MAGIC, ALU.add, ALU.subtract), [tq1])
                    tq3 = S.add("DVE", lambda e, _p=par, _dc=dc, _hf=hf, _w=wq_sb: e.tensor_scalar(
                        _w[:, _dc * NI + _hf * NH:_dc * NI + (_hf + 1) * NH],
                        wldf[_p][:, :], 1.0, None, ALU.mult), [tq2])
                    cross[f"wldf_free{par}"] = tq3
                    wq_toks[nm][dc][hf] = tq3
                    last_q = tq3
                drs_prev = last_q
        tscg = S.add("DVE", lambda e: e.tensor_scalar(
            wsceg[:, :], wmaxg[:, :], sxr[:, 1:2], None, ALU.mult), [dsxr, trg, rs_bounced[0]])
        tscu = S.add("DVE", lambda e: e.tensor_scalar(
            wsceu[:, :], wmaxu[:, :], sxr[:, 1:2], None, ALU.mult), [dsxr, tru, rs_bounced[1]])

        pfq.close()
        tzp = S.add("DVE", lambda e: e.memset(rpx[:, :], 0.0))
        l5 = {}
        for ic in range(ICN):
            for half in range(2):
                tb0 = half * 4
                mg = [None] * 4
                mu = [None] * 4
                for kc in range(DC):
                    for j in range(4):
                        tb = tb0 + j
                        mg[j] = S.add("PE", lambda e, _ic=ic, _kc=kc, _b=j, _tb=tb: e.matmul(
                            pb[_b][:, :], wqg[:, _kc * NI + _ic * 128:_kc * NI + _ic * 128 + 128],
                            xq[:, _kc * L + _tb * T:_kc * L + _tb * T + T],
                            start=(_kc == 0), stop=(_kc == DC - 1)),
                            [wq_toks["g"][kc],
                             txq_all if kc == 0 and j == 0 else None,
                             l5.get(f"pb{j}_free") if kc == 0 else None, mg[j]])
                    for j in range(4):
                        tb = tb0 + j
                        mu[j] = S.add("PE", lambda e, _ic=ic, _kc=kc, _b=4 + j, _tb=tb: e.matmul(
                            pb[_b][:, :], wqu[:, _kc * NI + _ic * 128:_kc * NI + _ic * 128 + 128],
                            xq[:, _kc * L + _tb * T:_kc * L + _tb * T + T],
                            start=(_kc == 0), stop=(_kc == DC - 1)),
                            [wq_toks["u"][kc],
                             l5.get(f"pb{4 + j}_free") if kc == 0 else None, mu[j]])
                for j in range(4):
                    tb = tb0 + j
                    ep = j % 2
                    tsg = S.add("ACT", lambda e, _b=j, _e=ep, _ic=ic: e.activation(
                        sgb[_e][:, :], pb[_b][:, :], AF.Silu, scale=wsceg[:, _ic:_ic + 1]),
                        [mg[j], tscg, l5.get(f"sgb_free{ep}")])
                    l5[f"pb{j}_free"] = tsg
                    tus = S.add("ACT", lambda e, _b=4 + j, _e=ep, _ic=ic: e.activation(
                        usb[_e][:, :], pb[_b][:, :], AF.Copy, scale=wsceu[:, _ic:_ic + 1]),
                        [mu[j], tscu, l5.get(f"usb_free{ep}")])
                    l5[f"pb{4 + j}_free"] = tus
                    tp = S.add("DVE", lambda e, _e=ep: e.tensor_tensor(
                        pbuf[_e][:, :], sgb[_e][:, :], usb[_e][:, :], ALU.mult),
                        [tsg, tus, l5.get(f"pbuf_free{ep}")])
                    l5[f"sgb_free{ep}"] = tp
                    l5[f"usb_free{ep}"] = tp
                    trp = S.add("DVE", lambda e, _e=ep: e.reduce_max(
                        sgb[_e][:, 0:1], pbuf[_e][:, :], axis=AX.X, apply_absolute_value=True), [tp])
                    trp2 = S.add("DVE", lambda e, _e=ep: e.tensor_tensor(
                        rpx[:, :], rpx[:, :], sgb[_e][:, 0:1], ALU.max),
                        [trp, tzp, l5.get("rpx_ser")])
                    l5["rpx_ser"] = trp2
                    _, dps = S.dma("ACT", p_h[ic * 128:(ic + 1) * 128, tb * T:tb * T + T],
                                   pbuf[ep][:, :], [tp])
                    l5[f"pbuf_free{ep}"] = dps
        pmax_dep = [l5["rpx_ser"], S.ring_token("ACT")]

        tg2 = allgather_scalar(rpx, cc_in2, cc_out2, SL_GM2, pmax_dep)
        tq1 = S.add("ACT", lambda e: e.activation(small[0:1, SL_T1:SL_T1 + 1],
                                                  small[0:1, SL_CD:SL_CD + 1], AF.Sigmoid), [t_cd])
        tq2 = S.add("DVE", lambda e: e.tensor_tensor(
            small[0:1, SL_SD:SL_SD + 1], small[0:1, SL_GM2:SL_GM2 + 1],
            small[0:1, SL_T1:SL_T1 + 1], ALU.mult), [tg2, tq1])
        tq3 = S.add("DVE", lambda e: e.tensor_scalar(
            small[0:1, SL_SD:SL_SD + 1], small[0:1, SL_SD:SL_SD + 1], 1.0 / 127.0, None, ALU.mult), [tq2])
        tq4 = S.add("DVE", lambda e: e.reciprocal(small[0:1, SL_RD:SL_RD + 1],
                                                  small[0:1, SL_SD:SL_SD + 1]), [tq3])
        tq6 = S.add("DVE", lambda e: e.tensor_scalar(
            small[0:1, SL_SD127:SL_SD127 + 1], small[0:1, SL_SD:SL_SD + 1], 1.0 / 127.0, None, ALU.mult), [tq3])
        _, dsd = S.dma("ACT", sc_h[0:1, 20:22],
                       ap3(small[0:1, :], SL_RD, [[SL_SD127 - SL_RD, 2], [1, 1]]), [tq4, tq6],
                       allow_slow_non_contiguous=True)
        _, dsdr = S.dma("SP", sxr[:, 2:4], rep128(sc_h[0:1, 20:22]), [dsd])

        ffn_ring = [S.ring_token("ACT"), S.ring_token("SP"), S.ring_token("POOL")]
        if stop_after == "P5":
            S.add("SP", lambda e: e.nop(nofuse=True), ffn_ring)
            S.emit(nc, ctx)
            return nc
        S.barrier()
        pf.close()

        p7 = ctx.enter_context(ExitStack())
        wqd = SB(p7, "wqd", [128, ICN * D], BF16)
        wld7 = [SB(p7, f"wld7{i}", [128, D]) for i in range(2)]
        rsrep7 = SB(p7, "rsrep7", [128, D])
        hq = [SB(p7, f"hq{i}", [128, ICN * T], BF16) for i in range(2)]
        pld = [SB(p7, f"pld{i}", [128, 4 * T]) for i in range(2)]
        x2rf = SB(p7, "x2rf", [128, DC * T])
        ob = [SB(p7, f"ob{i}", [128, T]) for i in range(2)]

        l7 = {}
        _, drs7 = S.dma("SP", rsrep7[:, :], rep128(hap(rsd_h, 0, [[1, D]])), [rs_bounced[2]])
        wd_toks = []
        for ic in range(ICN):
            par = ic % 2
            _, dwt = S.dma("POOL", wld7[par][:, :], wdT_h[ic * 128:(ic + 1) * 128, :],
                           [l7.get(f"wld7_free{par}")])
            tq1 = S.add("DVE", lambda e, _p=par: e.tensor_tensor(
                wld7[_p][:, :], wld7[_p][:, :], rsrep7[:, :], ALU.mult), [dwt, drs7])
            tq2 = S.add("DVE", lambda e, _p=par: e.tensor_scalar(
                wld7[_p][:, :], wld7[_p][:, :], MAGIC, MAGIC, ALU.add, ALU.subtract), [tq1])
            tq3 = S.add("DVE", lambda e, _p=par, _ic=ic: e.tensor_scalar(
                wqd[:, _ic * D:(_ic + 1) * D], wld7[_p][:, :], 1.0, None, ALU.mult), [tq2])
            l7[f"wld7_free{par}"] = tq3
            wd_toks.append(tq3)
        tcs = []
        for dc in range(DC):
            tc1 = S.add("DVE", lambda e, _d=dc: e.tensor_scalar(
                csc[:, _d:_d + 1], wmaxd[:, _d:_d + 1], sxr[:, 3:4], None, ALU.mult),
                [dsdr, trd, rs_bounced[2], tcs[-1] if tcs else None])
            tc2 = S.add("DVE", lambda e, _d=dc: e.tensor_tensor(
                csc[:, _d:_d + 1], csc[:, _d:_d + 1],
                modc[:, 5 * DC + _d:5 * DC + _d + 1], ALU.mult), [tc1, gfix_toks[1]])
            tcs.append(tc2)

        final_outs = []
        for tb in range(NCH):
            t0 = tb * T
            hq_toks = []
            for icg in range((ICN + 3) // 4):
                i0 = icg * 4
                ni_ = min(4, ICN - i0)
                par = icg % 2
                _, dpl = S.dma("POOL", pld[par][:, 0:ni_ * T],
                               hap(p_h, i0 * 128 * L + t0, [[L, 128], [128 * L, ni_], [1, T]]),
                               [l7.get(f"pld_free{par}")])
                th1 = S.add("DVE", lambda e, _p=par, _n=ni_: e.tensor_scalar(
                    pld[_p][:, 0:_n * T], pld[_p][:, 0:_n * T], sxr[:, 2:3], MAGIC,
                    ALU.mult, ALU.add), [dpl, dsdr])
                th2 = S.add("DVE", lambda e, _p=par, _n=ni_: e.tensor_scalar(
                    pld[_p][:, 0:_n * T], pld[_p][:, 0:_n * T], MAGIC, -128.0,
                    ALU.subtract, ALU.max), [th1])
                th3 = S.add("DVE", lambda e, _p=par, _n=ni_, _i0=i0, _hp=tb % 2: e.tensor_scalar(
                    hq[_hp][:, _i0 * T:(_i0 + _n) * T], pld[_p][:, 0:_n * T],
                    127.0, None, ALU.min), [th2, l7.get(f"hq_free{tb % 2}")])
                l7[f"pld_free{par}"] = th3
                hq_toks.append(th3)
            dx2rs = []
            for dc in range(DC):
                _, dx2r = S.dma("SP", x2rf[:, dc * T:(dc + 1) * T],
                                x2_h[dc * 128:(dc + 1) * 128, t0:t0 + T],
                                [l7.get("x2rf_free") if dc == 0 else None])
                dx2rs.append(dx2r)
            mm = [None] * DC
            for kc in range(ICN):
                for dc in range(DC):
                    mm[dc] = S.add("PE", lambda e, _dc=dc, _kc=kc, _hp=tb % 2: e.matmul(
                        pb[_dc][:, :], wqd[:, _kc * D + _dc * 128:_kc * D + _dc * 128 + 128],
                        hq[_hp][:, _kc * T:(_kc + 1) * T],
                        start=(_kc == 0), stop=(_kc == ICN - 1)),
                        [hq_toks[kc // 4] if dc == 0 else None, wd_toks[kc] if dc == 0 else None,
                         l7.get(f"pb{dc}_free") if kc == 0 else None, mm[dc]])
            for dc in range(DC):
                par = dc % 2
                tout = S.add("DVE", lambda e, _dc=dc, _p=par: e.scalar_tensor_tensor(
                    ob[_p][:, :], pb[_dc][:, :], csc[:, _dc:_dc + 1],
                    x2rf[:, _dc * T:(_dc + 1) * T],
                    ALU.mult, ALU.add), [mm[dc], dx2rs[dc], tcs[dc], l7.get(f"ob_free{par}")])
                l7[f"pb{dc}_free"] = tout
                _, dout = S.dma("ACT", out_h[dc * 128:(dc + 1) * 128, t0:t0 + T],
                                ob[par][:, :], [tout])
                l7[f"ob_free{par}"] = dout
                final_outs.append(dout)
            l7["x2rf_free"] = [S.last_item["DVE"]]
            l7[f"hq_free{tb % 2}"] = mm[-1]
        S.add("SP", lambda e: e.nop(nofuse=True), final_outs)
        p7.close()

        S.emit(nc, ctx)
    return nc


_NC_CACHE = {}


def kernel(**inputs):
    bf = ml_dtypes.bfloat16
    x = np.asarray(inputs["L0_attn_x"], dtype=np.float32)
    conv_w = np.asarray(inputs["L0_attn_conv_w"], dtype=np.float32)[:, 0, :]
    fm = lambda v, nch: np.ascontiguousarray(np.asarray(v, np.float32).reshape(nch, 128).T)

    w_in = np.asarray(inputs["L0_attn_W_in"], np.float32)
    wxp = np.asarray(inputs["L0_attn_W_xproj"], np.float32)
    wdt = np.asarray(inputs["L0_attn_W_dt"], np.float32)
    wout = np.asarray(inputs["L0_attn_W_out"], np.float32)
    wg = np.asarray(inputs["L0_ffn_w_gate"], np.float32)
    wu = np.asarray(inputs["L0_ffn_w_up"], np.float32)
    wd = np.asarray(inputs["L0_ffn_w_down"], np.float32)

    common = {
        "w_inT": np.ascontiguousarray(w_in.T),
        "convw": np.ascontiguousarray(
            conv_w.reshape(HC, 128, 3).transpose(1, 0, 2).reshape(128, HC * 3).astype(bf)),
        "convb": fm(inputs["L0_attn_conv_b"], HC),
        "wxpT": np.ascontiguousarray(wxp.T),
        "wdtT": np.ascontiguousarray(wdt.T),
        "bdt": fm(inputs["L0_attn_b_dt"], HC),
        "alog": np.asarray(inputs["L0_attn_A_log"], np.float32).reshape(1, NS),
        "dvec": fm(inputs["L0_attn_D"], HC),
        "woutT": np.ascontiguousarray(wout.T),
        "clipx": np.asarray(inputs["L0_ffn_clip_x"], np.float32).reshape(1, 1),
        "clipd": np.asarray(inputs["L0_ffn_clip_down"], np.float32).reshape(1, 1),
        "wg": np.ascontiguousarray(wg),
        "wu": np.ascontiguousarray(wu),
        "wd": np.ascontiguousarray(wd),
        "wgT": np.ascontiguousarray(wg.T),
        "wuT": np.ascontiguousarray(wu.T),
        "wdT": np.ascontiguousarray(wd.T),
        "wmod": np.ascontiguousarray(np.asarray(inputs["L0_wmod"], np.float32)),
        "onesb": np.ones((128, 1), dtype=bf),
    }
    cmat = np.asarray(inputs["L0_c"], np.float32)
    in_maps = []
    for b in range(NB):
        m = dict(common)
        m["x"] = np.ascontiguousarray(x[b].T)
        m["cvec"] = fm(cmat[b], DC)
        in_maps.append(m)

    if "nc" not in _NC_CACHE:
        _NC_CACHE["nc"] = build()
    nc = _NC_CACHE["nc"]
    res = run_bass_kernel_spmd(nc, in_maps, core_ids=list(range(NB)))
    out = np.empty((NB, L, D), dtype=np.float32)
    for b in range(NB):
        out[b] = np.asarray(res.results[b]["out"], dtype=np.float32).T
    return out
